# version 02 (final)
# baseline (speedup 1.0000x reference)
import contextlib
import numpy as np
import concourse.bass as bass
import concourse.mybir as mybir
from concourse.bass_utils import run_bass_kernel_spmd

F32 = mybir.dt.float32
BF16 = mybir.dt.bfloat16
U8 = mybir.dt.uint8
AF = mybir.ActivationFunctionType
ALU = mybir.AluOpType
AX = mybir.AxisListType

P = 128
T = 4096
D = 1024
KT_D = D // P

C_HALF = float(3 * (2 ** 21))
M_VELT = float(2 ** 22 + 1)
LN_EPS = 1e-5
DEBUG_L0 = False
SERIAL_L0 = True


class Prog:
    def __init__(self):
        self.nc = bass.Bass("TRN2", target_bir_lowering=False, debug=False)
        self.es = contextlib.ExitStack()
        self.c = {}
        self.h = {}
        for n in ["ld", "lda", "ldb", "ld2", "st", "sta", "stb", "st2",
                  "pe", "dve", "act", "pool", "evac", "qf"]:
            self.h[n] = self.es.enter_context(self.nc.semaphore(n))
            self.c[n] = 0
        self.dram = {}
        self.uid = 0

    def inc(self, name, amt=1):
        self.c[name] += amt
        return self.h[name], amt

    def bump(self, eng, name, amt=1):
        h, v = self.inc(name, amt)
        eng.sem_inc(h, amt)
        return v

    def cur(self, name):
        return self.c[name]

    def hd(self, name):
        return self.h[name]

    def inp(self, name, shape, dtype=F32):
        ap = self.nc.dram_tensor(name, list(shape), dtype, kind="ExternalInput").ap()
        self.dram[name] = ap
        return ap

    def outp(self, name, shape, dtype=F32):
        ap = self.nc.dram_tensor(name, list(shape), dtype, kind="ExternalOutput").ap()
        self.dram[name] = ap
        return ap

    def scratch(self, name, shape, dtype):
        ap = self.nc.dram_tensor(name, list(shape), dtype, kind="Internal").ap()
        self.dram[name] = ap
        return ap

    def sb(self, shape, dtype, es=None):
        self.uid += 1
        return (es or self.es).enter_context(
            self.nc.sbuf_tensor(f"sb{self.uid}", list(shape), dtype))

    def ps(self, shape, dtype=F32, es=None):
        self.uid += 1
        return (es or self.es).enter_context(
            self.nc.psum_tensor(f"ps{self.uid}", list(shape), dtype))

    def close(self):
        self.es.close()

    def constcol(self, val):
        key = float(val)
        if not hasattr(self, "_ccols"):
            self._ccols = {}
        if key not in self._ccols:
            t = self.sb([P, 1], F32)
            with self.nc.Block() as block:
                @block.gpsimd
                def _(g):
                    g.memset(t[:], key)
            self._ccols[key] = t
        return self._ccols[key]

    def consts(self):
        for cv in (LN_EPS, -0.5, -1.0):
            self.constcol(cv)
        self.ones_col16 = self.sb([P, 1], BF16)
        self.ones_row16 = self.sb([1, P], BF16)
        self.ones_row32 = self.sb([1, P], F32)
        self.ones_col32 = self.sb([P, 1], F32)
        self.id16 = self.sb([P, P], BF16)
        self.id32 = self.sb([P, P], F32)
        self.ones_pp16 = self.sb([P, P], BF16)
        self.ones_pp32 = self.sb([P, P], F32)
        with self.nc.Block() as block:
            @block.gpsimd
            def _(g):
                g.memset(self.ones_col16[:], 1.0)
                g.memset(self.ones_row16[:], 1.0)
                g.memset(self.ones_row32[:], 1.0)
                g.memset(self.ones_col32[:], 1.0)
                g.memset(self.ones_pp16[:], 1.0).then_inc(*self.inc("pool"))
                g.memset(self.ones_pp32[:], 1.0).then_inc(*self.inc("pool"))
                g.wait_ge(self.hd("pool"), self.cur("pool"))
                g.affine_select(self.id16[:], self.ones_pp16[:], [[-1, P]],
                                ALU.is_equal, 0.0, base=0, channel_multiplier=1)
                g.affine_select(self.id32[:], self.ones_pp32[:], [[-1, P]],
                                ALU.is_equal, 0.0, base=0, channel_multiplier=1)


def prep_weight(p, wT, K, O, row_scale=None, col_scale=None):
    nc = p.nc
    p.uid += 1
    out = p.scratch(f"w16_{p.uid}", [K, O], BF16)
    kt = K // P
    CH = min(O, 4096)
    och = O // CH
    assert O % CH == 0
    with contextlib.ExitStack() as es:
        f32 = p.sb([P, 2 * CH], F32, es)
        b16 = p.sb([P, 2 * CH], BF16, es)
        rs = p.sb([P, kt], F32, es) if row_scale is not None else None
        csf = p.sb([1, O], F32, es) if col_scale is not None else None
        cs_ps = p.ps([P, CH], F32, es) if col_scale is not None else None
        cs_sb = p.sb([P, CH], F32, es) if col_scale is not None else None
        if col_scale is not None:
            assert och == 1
        if row_scale is not None or col_scale is not None:
            with nc.Block() as block:
                @block.scalar
                def _(sc):
                    if row_scale is not None:
                        sc.dma_start(out=rs[:], in_=row_scale
                                     ).then_inc(*p.inc("ld2", 16))
                    if col_scale is not None:
                        sc.dma_start(out=csf[:], in_=col_scale
                                     ).then_inc(*p.inc("ld2", 16))
                    sc.wait_ge(p.hd("ld2"), p.cur("ld2"))
        if col_scale is not None:
            with nc.Block() as block:
                @block.tensor
                def _(t):
                    for c0 in range(0, CH, 512):
                        t.matmul(cs_ps[:, c0:c0 + 512], p.ones_row32[:],
                                 csf[:, c0:c0 + 512],
                                 start=True, stop=True).then_inc(*p.inc("pe"))

                @block.vector
                def _(v):
                    v.wait_ge(p.hd("pe"), p.cur("pe"))
                    v.tensor_copy(cs_sb[:], cs_ps[:])

        with nc.Block() as block:
            n_iter = kt * och
            dve0 = p.cur("dve")
            ld_p = [p.cur("lda"), p.cur("ldb")]
            st_p = [p.cur("sta"), p.cur("stb")]
            LD = ["lda", "ldb"]
            ST = ["sta", "stb"]

            @block.sync
            def _(sync):
                for i in range(n_iter):
                    k, oc = divmod(i, och)
                    if i >= 2:
                        sync.wait_ge(p.hd("dve"), dve0 + i - 1)
                    sync.dma_start(
                        out=f32[:, (i % 2) * CH:(i % 2 + 1) * CH],
                        in_=wT[k * P:(k + 1) * P, oc * CH:(oc + 1) * CH],
                    ).then_inc(*p.inc(LD[i % 2], 16))
                sync.wait_ge(p.hd("lda"), p.cur("lda"))
                sync.wait_ge(p.hd("ldb"), p.cur("ldb"))

            @block.vector
            def _(v):
                for i in range(n_iter):
                    k, oc = divmod(i, och)
                    v.wait_ge(p.hd(LD[i % 2]), ld_p[i % 2] + 16 * (i // 2 + 1))
                    if i >= 2:
                        v.wait_ge(p.hd(ST[i % 2]), st_p[i % 2] + 16 * (i // 2))
                    src = f32[:, (i % 2) * CH:(i % 2 + 1) * CH]
                    dst = b16[:, (i % 2) * CH:(i % 2 + 1) * CH]
                    if col_scale is not None:
                        v.tensor_tensor(dst, src, cs_sb[:], ALU.mult
                                        ).then_inc(*p.inc("dve"))
                    elif row_scale is not None:
                        v.tensor_scalar_mul(dst, src, rs[:, k:k + 1]
                                            ).then_inc(*p.inc("dve"))
                    else:
                        v.tensor_copy(dst, src).then_inc(*p.inc("dve"))

            @block.gpsimd
            def _(g):
                for i in range(n_iter):
                    k, oc = divmod(i, och)
                    g.wait_ge(p.hd("dve"), dve0 + i + 1)
                    g.dma_start(
                        out=out[k * P:(k + 1) * P, oc * CH:(oc + 1) * CH],
                        in_=b16[:, (i % 2) * CH:(i % 2 + 1) * CH],
                    ).then_inc(*p.inc(ST[i % 2], 16))
                g.wait_ge(p.hd("sta"), p.cur("sta"))
                g.wait_ge(p.hd("stb"), p.cur("stb"))
    return out


def load_cols(p, vec_dram, kt, es=None):
    cols = p.sb([P, kt], F32, es)
    with p.nc.Block() as block:
        @block.scalar
        def _(sc):
            sc.dma_start(out=cols[:], in_=vec_dram).then_inc(*p.inc("ld2", 16))
            sc.wait_ge(p.hd("ld2"), p.cur("ld2"))
    return cols


def ln_phase(p, r_in, Tt, out16=None, out32=None):
    nc = p.nc
    eps_c = p.constcol(LN_EPS)
    mhalf_c = p.constcol(-0.5)
    NTC = Tt // 512
    with contextlib.ExitStack() as es:
        xb = [p.sb([P, 512 * KT_D], F32, es) for _ in range(2)]
        sq = p.sb([P, 512 * KT_D], F32, es)
        ob32 = p.sb([P, 2 * 512], F32, es)
        ob16 = p.sb([P, 2 * 512], BF16, es)
        stat = p.ps([1, 1024], F32, es)
        bc = p.ps([P, 1024], F32, es)
        srow = p.sb([1, 1024 * NTC], F32, es)

        with nc.Block() as block:
            pe0, act0, dve0 = (p.cur("pe"), p.cur("act"), p.cur("dve"))
            LD = ["lda", "ldb"]
            ld_p = [p.cur("lda"), p.cur("ldb")]

            @block.sync
            def _(sync):
                for tc in range(NTC):
                    if tc >= 2:
                        sync.wait_ge(p.hd("pe"), pe0 + tc - 1)
                    for k in range(KT_D):
                        sync.dma_start(
                            out=xb[tc % 2][:, k * 512:(k + 1) * 512],
                            in_=r_in[k * P:(k + 1) * P, tc * 512:(tc + 1) * 512],
                        ).then_inc(*p.inc(LD[tc % 2], 16))
                sync.wait_ge(p.hd("lda"), p.cur("lda"))
                sync.wait_ge(p.hd("ldb"), p.cur("ldb"))

            @block.scalar
            def _(sc):
                for tc in range(NTC):
                    sc.wait_ge(p.hd(LD[tc % 2]), ld_p[tc % 2] + 16 * KT_D * (tc // 2 + 1))
                    if tc >= 1:
                        sc.wait_ge(p.hd("pe"), pe0 + tc)
                    for k in range(KT_D):
                        sc.activation(sq[:, k * 512:(k + 1) * 512],
                                      xb[tc % 2][:, k * 512:(k + 1) * 512],
                                      AF.Square).then_inc(*p.inc("act"))
                    sc.wait_ge(p.hd("dve"), dve0 + tc + 1)
                    sl = srow[:, tc * 1024 + 512:tc * 1024 + 1024]
                    sc.activation(sl, sl, AF.Ln, bias=eps_c[0:1, :])
                    sc.drain()
                    sc.activation(sl, sl, AF.Exp, scale=mhalf_c[0:1, :]
                                  ).then_inc(*p.inc("act"))

            @block.tensor
            def _(t):
                for tc in range(NTC):
                    t.wait_ge(p.hd(LD[tc % 2]), ld_p[tc % 2] + 16 * KT_D * (tc // 2 + 1))
                    if tc >= 1:
                        t.wait_ge(p.hd("dve"), dve0 + tc)
                    for k in range(KT_D):
                        t.matmul(stat[:, 0:512], p.ones_col32[:],
                                 xb[tc % 2][:, k * 512:(k + 1) * 512],
                                 start=(k == 0), stop=(k == KT_D - 1))
                    for k in range(KT_D):
                        t.wait_ge(p.hd("act"), act0 + tc * (KT_D + 1) + k + 1)
                        mm = t.matmul(stat[:, 512:1024], p.ones_col32[:],
                                      sq[:, k * 512:(k + 1) * 512],
                                      start=(k == 0), stop=(k == KT_D - 1))
                    mm.then_inc(*p.inc("pe"))

            @block.vector
            def _(v):
                for tc in range(NTC):
                    v.wait_ge(p.hd("pe"), pe0 + tc + 1)
                    nm = srow[:, tc * 1024:tc * 1024 + 512]
                    vr = srow[:, tc * 1024 + 512:tc * 1024 + 1024]
                    v.tensor_scalar_mul(nm, stat[:, 0:512], -1.0 / D)
                    v.drain()
                    v.tensor_tensor(vr, nm, nm, ALU.mult)
                    v.drain()
                    v.scalar_tensor_tensor(vr, stat[:, 512:1024], 1.0 / D, vr,
                                           ALU.mult, ALU.subtract
                                           ).then_inc(*p.inc("dve"))

        with nc.Block() as block:
            pe0, dve0 = p.cur("pe"), p.cur("dve")
            LD = ["lda", "ldb"]
            ST = ["sta", "stb"]
            ld_p = [p.cur("lda"), p.cur("ldb")]
            st_p = [p.cur("sta"), p.cur("stb")]
            n_out = (1 if out16 is not None else 0) + (1 if out32 is not None else 0)

            @block.sync
            def _(sync):
                for tc in range(NTC):
                    if tc >= 2:
                        sync.wait_ge(p.hd("dve"), dve0 + (tc - 1) * KT_D)
                    for k in range(KT_D):
                        sync.dma_start(
                            out=xb[tc % 2][:, k * 512:(k + 1) * 512],
                            in_=r_in[k * P:(k + 1) * P, tc * 512:(tc + 1) * 512],
                        ).then_inc(*p.inc(LD[tc % 2], 16))
                sync.wait_ge(p.hd("lda"), p.cur("lda"))
                sync.wait_ge(p.hd("ldb"), p.cur("ldb"))

            @block.tensor
            def _(t):
                for tc in range(NTC):
                    if tc >= 1:
                        t.wait_ge(p.hd("dve"), dve0 + tc * KT_D)
                    t.matmul(bc[:, 0:512], p.ones_row32[:],
                             srow[:, tc * 1024:tc * 1024 + 512],
                             start=True, stop=True)
                    t.matmul(bc[:, 512:1024], p.ones_row32[:],
                             srow[:, tc * 1024 + 512:tc * 1024 + 1024],
                             start=True, stop=True).then_inc(*p.inc("pe"))

            @block.vector
            def _(v):
                for tc in range(NTC):
                    v.wait_ge(p.hd(LD[tc % 2]), ld_p[tc % 2] + 16 * KT_D * (tc // 2 + 1))
                    v.wait_ge(p.hd("pe"), pe0 + tc + 1)
                    for k in range(KT_D):
                        j = tc * KT_D + k
                        b = j % 2
                        xk = xb[tc % 2][:, k * 512:(k + 1) * 512]
                        o32 = ob32[:, b * 512:(b + 1) * 512]
                        o16 = ob16[:, b * 512:(b + 1) * 512]
                        if j >= 2:
                            v.wait_ge(p.hd(ST[j % 2]), st_p[j % 2] + 16 * n_out * (j // 2))
                        v.tensor_tensor(o32, xk, bc[:, 0:512], ALU.add)
                        v.drain()
                        inst = v.tensor_tensor(o32, o32, bc[:, 512:1024], ALU.mult)
                        if out16 is not None:
                            v.drain()
                            inst = v.tensor_copy(o16, o32)
                        inst.then_inc(*p.inc("dve"))

            @block.gpsimd
            def _(g):
                for tc in range(NTC):
                    for k in range(KT_D):
                        j = tc * KT_D + k
                        b = j % 2
                        g.wait_ge(p.hd("dve"), dve0 + j + 1)
                        if out16 is not None:
                            g.dma_start(
                                out=out16[k * P:(k + 1) * P, tc * 512:(tc + 1) * 512],
                                in_=ob16[:, b * 512:(b + 1) * 512]
                            ).then_inc(*p.inc(ST[j % 2], 16))
                        if out32 is not None:
                            g.dma_start(
                                out=out32[k * P:(k + 1) * P, tc * 512:(tc + 1) * 512],
                                in_=ob32[:, b * 512:(b + 1) * 512]
                            ).then_inc(*p.inc(ST[j % 2], 16))
                g.wait_ge(p.hd("sta"), p.cur("sta"))
                g.wait_ge(p.hd("stb"), p.cur("stb"))


def mm_stage(p, streams, K, O, act_tiles, TG, evac, evac_done="vector",
             use_scalar=False, use_vector=True, use_gpsimd=True, o_group=512):
    nc = p.nc
    kt = K // P
    W = len(streams)
    og = o_group
    n_og = O // og
    assert O % og == 0 and og % P == 0
    opg = og // P
    ntc = TG // 512
    n_iter = n_og * opg * ntc
    gps = opg * ntc

    with contextlib.ExitStack() as es:
        wbuf = [[p.sb([P, kt * og], BF16, es) for _ in range(2)] for _ in range(W)]
        psums = [[p.ps([P, 512], F32, es) for _ in range(2)] for _ in range(W)]
        pe0, ev0 = p.cur("pe"), p.cur("evac")
        LD = ["lda", "ldb"]
        ld_p = [p.cur("lda"), p.cur("ldb")]

        def it_of(i):
            s, r = divmod(i, opg * ntc)
            ot, tc = divmod(r, ntc)
            return s, ot, tc

        with nc.Block() as block:
            @block.sync
            def _(sync):
                for s in range(n_og):
                    if s >= 2:
                        sync.wait_ge(p.hd("pe"), pe0 + (s - 1) * gps * W)
                    for w in range(W):
                        for k in range(kt):
                            sync.dma_start(
                                out=wbuf[w][s % 2][:, k * og:(k + 1) * og],
                                in_=streams[w][k * P:(k + 1) * P, s * og:(s + 1) * og],
                            ).then_inc(*p.inc(LD[s % 2], 16))
                sync.wait_ge(p.hd("lda"), p.cur("lda"))
                sync.wait_ge(p.hd("ldb"), p.cur("ldb"))

            @block.tensor
            def _(t):
                for i in range(n_iter):
                    s, ot, tc = it_of(i)
                    if ot == 0 and tc == 0:
                        t.wait_ge(p.hd(LD[s % 2]),
                                  ld_p[s % 2] + 16 * kt * W * (s // 2 + 1))
                    if i >= 2:
                        t.wait_ge(p.hd("evac"), ev0 + i - 1)
                    for w in range(W):
                        for k in range(kt):
                            mm = t.matmul(
                                psums[w][i % 2][:],
                                wbuf[w][s % 2][:, k * og + ot * P:k * og + (ot + 1) * P],
                                act_tiles[k][:, tc * 512:(tc + 1) * 512],
                                start=(k == 0), stop=(k == kt - 1))
                        mm.then_inc(*p.inc("pe"))

            if use_scalar:
                @block.scalar
                def _(sc):
                    for i in range(n_iter):
                        s, ot, tc = it_of(i)
                        sc.wait_ge(p.hd("pe"), pe0 + (i + 1) * W)
                        r = evac("scalar", sc, dict(
                            i=i, o=s * opg + ot, tc=tc,
                            psums=[psums[w][i % 2] for w in range(W)]))
                        if evac_done == "scalar":
                            r.then_inc(*p.inc("evac"))

            if use_vector:
                @block.vector
                def _(v):
                    for i in range(n_iter):
                        s, ot, tc = it_of(i)
                        v.wait_ge(p.hd("pe"), pe0 + (i + 1) * W)
                        r = evac("vector", v, dict(
                            i=i, o=s * opg + ot, tc=tc,
                            psums=[psums[w][i % 2] for w in range(W)]))
                        if evac_done == "vector":
                            r.then_inc(*p.inc("evac"))

            if use_gpsimd:
                @block.gpsimd
                def _(g):
                    for i in range(n_iter):
                        s, ot, tc = it_of(i)
                        evac("gpsimd", g, dict(
                            i=i, o=s * opg + ot, tc=tc, psums=None))
                    g.wait_ge(p.hd("st"), p.cur("st"))
                    g.wait_ge(p.hd("sta"), p.cur("sta"))
                    g.wait_ge(p.hd("stb"), p.cur("stb"))


def emit_l1_ffn(p, r_in, r_out, wd):
    nc = p.nc
    H = 4096
    w1 = prep_weight(p, wd["w1T"], D, H)
    w2 = prep_weight(p, wd["w2T"], D, H)
    h1 = prep_weight(p, wd["h1T"], H, H)
    h2 = prep_weight(p, wd["h2T"], H, H)
    w3 = prep_weight(p, wd["w3T"], H, D, col_scale=wd["s3"])
    p.uid += 1
    xln = p.scratch(f"xln{p.uid}", [D, T], BF16)
    ln_phase(p, r_in, T, out16=xln)
    s1c = load_cols(p, wd["s1c"], H // P)
    s2c = load_cols(p, wd["s2c"], H // P)
    hs1c = load_cols(p, wd["hs1c"], H // P)
    hs2c = load_cols(p, wd["hs2c"], H // P)

    TG = 1024
    for tg in range(T // TG):
        with contextlib.ExitStack() as es:
            a1 = p.sb([P, (H // P) * TG], BF16, es)
            a2 = p.sb([P, (H // P) * TG], BF16, es)
            tmp = p.sb([P, 2 * 512], F32, es)
            ob = p.sb([P, 2 * 512], F32, es)
            a1_t = [a1[:, k * TG:(k + 1) * TG] for k in range(H // P)]
            a2_t = [a2[:, k * TG:(k + 1) * TG] for k in range(H // P)]

            es_b = contextlib.ExitStack()
            xg = p.sb([P, KT_D * TG], BF16, es_b)
            xg_t = [xg[:, k * TG:(k + 1) * TG] for k in range(KT_D)]

            with nc.Block() as block:
                @block.sync
                def _(sync):
                    for k in range(KT_D):
                        sync.dma_start(out=xg[:, k * TG:(k + 1) * TG],
                                       in_=xln[k * P:(k + 1) * P, tg * TG:(tg + 1) * TG]
                                       ).then_inc(*p.inc("ld", 16))
                    sync.wait_ge(p.hd("ld"), p.cur("ld"))

            act_marks = {}
            evB0 = p.cur("evac")

            def evB(kind, eng, it):
                i, o, tc = it["i"], it["o"], it["tc"]
                sl = slice((i % 2) * 512, (i % 2 + 1) * 512)
                if kind == "scalar":
                    if i >= 2:
                        eng.wait_ge(p.hd("evac"), evB0 + i - 1)
                    eng.activation(tmp[:, sl], it["psums"][0][:], AF.Silu,
                                   scale=s1c[:, o:o + 1]).then_inc(*p.inc("act"))
                    act_marks[i] = p.cur("act")
                    return None
                if kind == "vector":
                    eng.wait_ge(p.hd("act"), act_marks[i])
                    return eng.scalar_tensor_tensor(
                        a1[:, o * TG + tc * 512:o * TG + (tc + 1) * 512],
                        it["psums"][1][:], s2c[:, o:o + 1], tmp[:, sl],
                        ALU.mult, ALU.mult)
                return None

            mm_stage(p, [w1, w2], D, H, xg_t, TG, evB, evac_done="vector",
                     use_scalar=True, use_gpsimd=False, o_group=512)
            es_b.close()

            def mk_gelu(dst, scol):
                def ev(kind, eng, it):
                    i, o, tc = it["i"], it["o"], it["tc"]
                    if kind == "scalar":
                        return eng.activation(
                            dst[:, o * TG + tc * 512:o * TG + (tc + 1) * 512],
                            it["psums"][0][:], AF.Gelu, scale=scol[:, o:o + 1])
                    return None
                return ev

            mm_stage(p, [h1], H, H, a1_t, TG, mk_gelu(a2, hs1c),
                     evac_done="scalar", use_scalar=True, use_vector=False,
                     use_gpsimd=False, o_group=256)

            mm_stage(p, [h2], H, H, a2_t, TG, mk_gelu(a1, hs2c),
                     evac_done="scalar", use_scalar=True, use_vector=False,
                     use_gpsimd=False, o_group=256)

            es_e = contextlib.ExitStack()
            rin = p.sb([P, KT_D * TG], F32, es_e)
            with nc.Block() as block:
                @block.sync
                def _(sync):
                    for k in range(KT_D):
                        sync.dma_start(out=rin[:, k * TG:(k + 1) * TG],
                                       in_=r_in[k * P:(k + 1) * P, tg * TG:(tg + 1) * TG]
                                       ).then_inc(*p.inc("ld", 16))
                    sync.wait_ge(p.hd("ld"), p.cur("ld"))

            evE0 = p.cur("evac")
            STp = ["sta", "stb"]
            stE0 = [p.cur("sta"), p.cur("stb")]

            def evE(kind, eng, it):
                i, o, tc = it["i"], it["o"], it["tc"]
                sl = slice((i % 2) * 512, (i % 2 + 1) * 512)
                if kind == "vector":
                    if i >= 2:
                        eng.wait_ge(p.hd(STp[i % 2]), stE0[i % 2] + 16 * (i // 2))
                    return eng.scalar_tensor_tensor(
                        ob[:, sl], it["psums"][0][:], 0.5,
                        rin[:, o * TG + tc * 512:o * TG + (tc + 1) * 512],
                        ALU.mult, ALU.add)
                if kind == "gpsimd":
                    eng.wait_ge(p.hd("evac"), evE0 + i + 1)
                    eng.dma_start(
                        out=r_out[o * P:(o + 1) * P,
                                  tg * TG + tc * 512:tg * TG + (tc + 1) * 512],
                        in_=ob[:, sl]).then_inc(*p.inc(STp[i % 2], 16))
                return None

            mm_stage(p, [w3], H, D, a1_t, TG, evE, evac_done="vector",
                     use_gpsimd=True, o_group=128)
            es_e.close()


def quant_ops(p, v, src, dst16, F, bufs):
    nb = F // 16
    bm, s, rc = bufs["bm"], bufs["s"], bufs["rc"]
    xn, qlg, dbuf, m8 = bufs["xn"], bufs["qlg"], bufs["dbuf"], bufs["m8"]

    def vw(buf, w):
        return buf[:, 0:nb * w].rearrange("p (a b) -> p a b", b=w)

    v.drain()
    v.tensor_scalar_mul(dbuf[:, 0:F], src, -1.0)
    v.drain()
    v.tensor_tensor(xn[:, 0:F], src, dbuf[:, 0:F], ALU.max)
    v.drain()
    x16 = vw(xn, 16)
    v.tensor_tensor(vw(qlg, 8), x16[:, :, 0:8], x16[:, :, 8:16], ALU.max)
    v.drain()
    q8 = vw(qlg, 8)
    v.tensor_tensor(vw(dbuf, 4), q8[:, :, 0:4], q8[:, :, 4:8], ALU.max)
    v.drain()
    d4 = vw(dbuf, 4)
    v.tensor_tensor(vw(qlg, 2), d4[:, :, 0:2], d4[:, :, 2:4], ALU.max)
    v.drain()
    q2 = vw(qlg, 2)
    v.tensor_tensor(vw(bm, 1), q2[:, :, 0:1], q2[:, :, 1:2], ALU.max)
    v.drain()
    v.tensor_scalar(s[:, 0:nb], bm[:, 0:nb], 1.0 / 6.0, 1e-12, ALU.mult, ALU.max)
    v.drain()
    v.reciprocal(rc[:, 0:nb], s[:, 0:nb])
    v.drain()
    rb = rc[:, 0:nb].unsqueeze(2).broadcast_to([P, nb, 16])
    sb = s[:, 0:nb].unsqueeze(2).broadcast_to([P, nb, 16])
    src3 = src.rearrange("p (a b) -> p a b", b=16)
    v.tensor_tensor(vw(xn, 16), src3, rb, ALU.mult)
    v.drain()
    v.tensor_scalar(qlg[:, 0:F], xn[:, 0:F], C_HALF, C_HALF, ALU.add, ALU.subtract)
    v.scalar_tensor_tensor(dbuf[:, 0:F], xn[:, 0:F], M_VELT, xn[:, 0:F],
                           ALU.mult, ALU.subtract)
    v.drain()
    v.scalar_tensor_tensor(dbuf[:, 0:F], xn[:, 0:F], M_VELT, dbuf[:, 0:F],
                           ALU.mult, ALU.subtract)
    v.drain()
    v.tensor_tensor(xn[:, 0:F], qlg[:, 0:F], qlg[:, 0:F], ALU.mult)
    v.drain()
    v.tensor_single_scalar(m8[:, 0:F], xn[:, 0:F], 4.0, ALU.is_ge)
    v.drain()
    v.copy_predicated(qlg[:, 0:F], m8[:, 0:F], dbuf[:, 0:F])
    v.drain()
    return v.tensor_tensor(dst16.rearrange("p (a b) -> p a b", b=16),
                           qlg[:, 0:F].rearrange("p (a b) -> p a b", b=16),
                           sb, ALU.mult)


def mk_quant_bufs(p, F, es):
    return {
        "bm": p.sb([P, F // 16], F32, es), "s": p.sb([P, F // 16], F32, es),
        "rc": p.sb([P, F // 16], F32, es), "xn": p.sb([P, F], F32, es),
        "qlg": p.sb([P, F], F32, es), "dbuf": p.sb([P, F], F32, es),
        "m8": p.sb([P, F], U8, es),
    }


def prep_quant_weight(p, w, R, C):
    nc = p.nc
    p.uid += 1
    out = p.scratch(f"wq16_{p.uid}", [C, R], BF16)
    rt = R // P
    ct = C // P
    with contextlib.ExitStack() as es:
        f32 = p.sb([P, C], F32, es)
        q16 = p.sb([P, C], BF16, es)
        stg = p.sb([P, C], BF16, es)
        qb2p = [mk_quant_bufs(p, C, es) for _ in range(2)]
        pst = [p.ps([P, 1024], BF16, es) for _ in range(2)]
        for r in range(rt):
            pe0, dve0 = p.cur("pe"), p.cur("dve")
            with nc.Block() as block:
                @block.sync
                def _(sync):
                    sync.dma_start(out=f32[:], in_=w[r * P:(r + 1) * P, :]
                                   ).then_inc(*p.inc("ld", 16))
                    sync.wait_ge(p.hd("ld"), p.cur("ld"))

                @block.vector
                def _(v):
                    v.wait_ge(p.hd("ld"), p.cur("ld"))
                    quant_ops(p, v, f32[:], q16[:], C, qb2p[r % 2]
                              ).then_inc(*p.inc("dve"))
                    for c in range(ct):
                        v.wait_ge(p.hd("pe"), pe0 + c + 1)
                        v.tensor_copy(stg[:, c * P:(c + 1) * P],
                                      pst[c % 2][:, 0:P]).then_inc(*p.inc("dve"))

                @block.tensor
                def _(t):
                    t.wait_ge(p.hd("dve"), dve0 + 1)
                    for c in range(ct):
                        if c >= 2:
                            t.wait_ge(p.hd("dve"), dve0 + 1 + c - 1)
                        t.transpose(pst[c % 2][:, 0:P],
                                    q16[:, c * P:(c + 1) * P],
                                    p.id16[:]).then_inc(*p.inc("pe"))

                @block.gpsimd
                def _(g):
                    g.wait_ge(p.hd("dve"), p.cur("dve"))
                    for c in range(ct):
                        g.dma_start(out=out[c * P:(c + 1) * P, r * P:(r + 1) * P],
                                    in_=stg[:, c * P:(c + 1) * P]
                                    ).then_inc(*p.inc("st", 16))
                    g.wait_ge(p.hd("st"), p.cur("st"))
    return out


def emit_l0_ffn(p, r_in, r_out, wd, preq=None):
    nc = p.nc
    I = 2816
    IT = I // P
    if preq is None:
        preq = {
            "g": prep_quant_weight(p, wd["wg"], I, D),
            "u": prep_quant_weight(p, wd["wu"], I, D),
            "d": prep_quant_weight(p, wd["wdn"], D, I),
        }
    wgq, wuq, wdq = preq["g"], preq["u"], preq["d"]
    p.uid += 1
    ln32 = p.scratch(f"l0ln{p.uid}", [D, T], F32)
    xqT = p.scratch(f"l0xq{p.uid}", [D, T], BF16)
    hqT = p.scratch(f"l0hq{p.uid}", [I, T], BF16)
    ln_phase(p, r_in, T, out32=ln32)

    with contextlib.ExitStack() as es:
        lnb = p.sb([P, 2 * D], F32, es)
        xq16 = p.sb([P, 2 * D], BF16, es)
        stg = p.sb([P, 2 * P], BF16, es)
        qb2x = [mk_quant_bufs(p, D, es) for _ in range(4)]
        ps32 = [p.ps([P, 512], F32, es) for _ in range(2)]
        ps16 = [p.ps([P, 1024], BF16, es) for _ in range(2)]
        dve0, pe0 = p.cur("dve"), p.cur("pe")
        LD = ["lda", "ldb"]
        ST = ["sta", "stb"]
        ld_p = [p.cur("lda"), p.cur("ldb")]
        st_p = [p.cur("sta"), p.cur("stb")]
        DVE_TT = 2 * KT_D + 1
        PE_TT = 2 * KT_D
        with nc.Block() as block:
            @block.tensor
            def _(t):
                for tt in range(T // P):
                    for k in range(KT_D):
                        j = tt * PE_TT + k
                        if k == 0:
                            t.wait_ge(p.hd(LD[tt % 2]),
                                      ld_p[tt % 2] + 16 * KT_D * (tt // 2 + 1))
                        if j >= 2:
                            t.wait_ge(p.hd("dve"), dve0 + tt * DVE_TT + k - 1)
                        t.transpose(ps32[k % 2][:, 0:P],
                                    lnb[:, (tt % 2) * D + k * P:(tt % 2) * D + (k + 1) * P],
                                    p.id32[:]).then_inc(*p.inc("pe"))
                    t.wait_ge(p.hd("dve"), dve0 + tt * DVE_TT + KT_D + 1)
                    for k in range(KT_D):
                        if k >= 2:
                            t.wait_ge(p.hd("dve"),
                                      dve0 + tt * DVE_TT + KT_D + 1 + k - 1)
                        t.transpose(ps16[k % 2][:, 0:P],
                                    xq16[:, (tt % 2) * D + k * P:(tt % 2) * D + (k + 1) * P],
                                    p.id16[:]).then_inc(*p.inc("pe"))

            @block.sync
            def _(sync):
                for tt in range(T // P):
                    if tt >= 2:
                        sync.wait_ge(p.hd("pe"), pe0 + (tt - 1) * PE_TT + KT_D)
                    for k in range(KT_D):
                        sync.dma_start(
                            out=lnb[:, (tt % 2) * D + k * P:(tt % 2) * D + (k + 1) * P],
                            in_=ln32[k * P:(k + 1) * P, tt * P:(tt + 1) * P],
                        ).then_inc(*p.inc(LD[tt % 2], 16))
                sync.wait_ge(p.hd("lda"), p.cur("lda"))
                sync.wait_ge(p.hd("ldb"), p.cur("ldb"))

            @block.vector
            def _(v):
                for tt in range(T // P):
                    for k in range(KT_D):
                        v.wait_ge(p.hd("pe"), pe0 + tt * PE_TT + k + 1)
                        v.tensor_copy(
                            lnb[:, (tt % 2) * D + k * P:(tt % 2) * D + (k + 1) * P],
                            ps32[k % 2][:, 0:P]
                        ).then_inc(*p.inc("dve"))
                    if tt >= 2:
                        v.wait_ge(p.hd("pe"), pe0 + (tt - 1) * PE_TT + PE_TT)
                    v.wait_ge(p.hd("dve"), p.cur("dve"))
                    quant_ops(p, v, lnb[:, (tt % 2) * D:(tt % 2 + 1) * D],
                              xq16[:, (tt % 2) * D:(tt % 2 + 1) * D], D,
                              qb2x[tt % 4]).then_inc(*p.inc("dve"))
                    for k in range(KT_D):
                        j2 = tt * PE_TT + KT_D + k
                        v.wait_ge(p.hd("pe"), pe0 + j2 + 1)
                        jj = tt * KT_D + k
                        if jj >= 2:
                            v.wait_ge(p.hd(ST[jj % 2]), st_p[jj % 2] + 16 * (jj // 2))
                        v.tensor_copy(stg[:, (jj % 2) * P:(jj % 2 + 1) * P],
                                      ps16[k % 2][:, 0:P]
                                      ).then_inc(*p.inc("dve"))

            @block.gpsimd
            def _(g):
                for tt in range(T // P):
                    for k in range(KT_D):
                        jj = tt * KT_D + k
                        g.wait_ge(p.hd("dve"),
                                  dve0 + tt * DVE_TT + KT_D + 1 + k + 1)
                        g.dma_start(out=xqT[k * P:(k + 1) * P, tt * P:(tt + 1) * P],
                                    in_=stg[:, (jj % 2) * P:(jj % 2 + 1) * P]
                                    ).then_inc(*p.inc(ST[jj % 2], 16))
                g.wait_ge(p.hd("sta"), p.cur("sta"))
                g.wait_ge(p.hd("stb"), p.cur("stb"))

    h32d = p.scratch(f"l0h32{p.uid}", [T, 2816], F32)
    with contextlib.ExitStack() as es:
        xq_sb = p.sb([P, KT_D * T], BF16, es)
        with nc.Block() as block:
            @block.sync
            def _(sync):
                for k in range(KT_D):
                    sync.dma_start(out=xq_sb[:, k * T:(k + 1) * T],
                                   in_=xqT[k * P:(k + 1) * P, :]
                                   ).then_inc(*p.inc("ld", 16))
                sync.wait_ge(p.hd("ld"), p.cur("ld"))

        OCH = [(0, 512), (512, 512), (1024, 512), (1536, 512), (2048, 512),
               (2560, 256)]
        for (o0, ow) in OCH:
            with contextlib.ExitStack() as es2:
                wgb = p.sb([P, KT_D * ow], BF16, es2)
                wub = p.sb([P, KT_D * ow], BF16, es2)
                sil = p.sb([P, 2 * ow], F32, es2)
                ob = p.sb([P, 2 * ow], F32, es2)
                psg = p.ps([P, 512], F32, es2)
                psu = p.ps([P, 512], F32, es2)
                dve0, pe0, act0 = p.cur("dve"), p.cur("pe"), p.cur("act")
                ST = ["sta", "stb"]
                st_p = [p.cur("sta"), p.cur("stb")]
                ld0 = p.cur("ld")
                with nc.Block() as block:
                    @block.sync
                    def _(sync):
                        for k in range(KT_D):
                            sync.dma_start(out=wgb[:, k * ow:(k + 1) * ow],
                                           in_=wgq[k * P:(k + 1) * P, o0:o0 + ow]
                                           ).then_inc(*p.inc("ld", 16))
                            sync.dma_start(out=wub[:, k * ow:(k + 1) * ow],
                                           in_=wuq[k * P:(k + 1) * P, o0:o0 + ow]
                                           ).then_inc(*p.inc("ld", 16))
                        sync.wait_ge(p.hd("ld"), p.cur("ld"))

                    @block.tensor
                    def _(t):
                        t.wait_ge(p.hd("ld"), ld0 + 16 * 2 * KT_D)
                        for tt in range(T // P):
                            if tt >= 1:
                                t.wait_ge(p.hd("dve"), dve0 + tt)
                            for k in range(KT_D):
                                t.matmul(psg[:, 0:ow],
                                         xq_sb[:, k * T + tt * P:k * T + (tt + 1) * P],
                                         wgb[:, k * ow:(k + 1) * ow],
                                         start=(k == 0), stop=(k == KT_D - 1))
                            for k in range(KT_D):
                                mm = t.matmul(psu[:, 0:ow],
                                              xq_sb[:, k * T + tt * P:k * T + (tt + 1) * P],
                                              wub[:, k * ow:(k + 1) * ow],
                                              start=(k == 0), stop=(k == KT_D - 1))
                            mm.then_inc(*p.inc("pe"))

                    @block.scalar
                    def _(sc):
                        for tt in range(T // P):
                            sc.wait_ge(p.hd("pe"), pe0 + tt + 1)
                            if tt >= 2:
                                sc.wait_ge(p.hd("dve"), dve0 + tt - 1)
                            sc.activation(sil[:, (tt % 2) * ow:(tt % 2) * ow + ow],
                                          psg[:, 0:ow], AF.Silu
                                          ).then_inc(*p.inc("act"))

                    @block.vector
                    def _(v):
                        for tt in range(T // P):
                            v.wait_ge(p.hd("act"), act0 + tt + 1)
                            v.wait_ge(p.hd("pe"), pe0 + tt + 1)
                            if tt >= 2:
                                v.wait_ge(p.hd(ST[tt % 2]), st_p[tt % 2] + 16 * (tt // 2))
                            v.tensor_tensor(ob[:, (tt % 2) * ow:(tt % 2) * ow + ow],
                                            sil[:, (tt % 2) * ow:(tt % 2) * ow + ow],
                                            psu[:, 0:ow], ALU.mult
                                            ).then_inc(*p.inc("dve"))

                    @block.gpsimd
                    def _(g):
                        for tt in range(T // P):
                            g.wait_ge(p.hd("dve"), dve0 + tt + 1)
                            g.dma_start(
                                out=h32d[tt * P:(tt + 1) * P, o0:o0 + ow],
                                in_=ob[:, (tt % 2) * ow:(tt % 2) * ow + ow]
                            ).then_inc(*p.inc(ST[tt % 2], 16))
                        g.wait_ge(p.hd("sta"), p.cur("sta"))
                        g.wait_ge(p.hd("stb"), p.cur("stb"))

    QCH = [(0, 512), (512, 512), (1024, 512), (1536, 512), (2048, 512),
           (2560, 256)]
    IT22 = 22
    with contextlib.ExitStack() as es:
        hb = p.sb([P, 2 * 2816], F32, es)
        hq16 = p.sb([P, 2 * 2816], BF16, es)
        stg = p.sb([P, 2 * P], BF16, es)
        qbs = [mk_quant_bufs(p, 512, es) for _ in range(4)]
        ps16h = [p.ps([P, 1024], BF16, es) for _ in range(2)]
        dve0, pe0 = p.cur("dve"), p.cur("pe")
        LD = ["lda", "ldb"]
        ST = ["sta", "stb"]
        ld_p = [p.cur("lda"), p.cur("ldb")]
        st_p = [p.cur("sta"), p.cur("stb")]
        NQ = len(QCH)
        DVE_TT = NQ + IT22
        with nc.Block() as block:
            @block.sync
            def _(sync):
                for tt in range(T // P):
                    if tt >= 2:
                        sync.wait_ge(p.hd("dve"), dve0 + (tt - 2) * DVE_TT + NQ)
                    sync.dma_start(out=hb[:, (tt % 2) * 2816:(tt % 2 + 1) * 2816],
                                   in_=h32d[tt * P:(tt + 1) * P, :]
                                   ).then_inc(*p.inc(LD[tt % 2], 16))
                sync.wait_ge(p.hd("lda"), p.cur("lda"))
                sync.wait_ge(p.hd("ldb"), p.cur("ldb"))

            @block.vector
            def _(v):
                for tt in range(T // P):
                    v.wait_ge(p.hd(LD[tt % 2]), ld_p[tt % 2] + 16 * (tt // 2 + 1))
                    if tt >= 2:
                        v.wait_ge(p.hd("pe"), pe0 + tt * IT22)
                    for qi, (q0, qw) in enumerate(QCH):
                        quant_ops(p, v,
                                  hb[:, (tt % 2) * 2816 + q0:(tt % 2) * 2816 + q0 + qw],
                                  hq16[:, (tt % 2) * 2816 + q0:(tt % 2) * 2816 + q0 + qw],
                                  qw, qbs[qi % 4]).then_inc(*p.inc("dve"))
                    for c in range(IT22):
                        j = tt * IT22 + c
                        v.wait_ge(p.hd("pe"), pe0 + j + 1)
                        if j >= 2:
                            v.wait_ge(p.hd(ST[j % 2]), st_p[j % 2] + 16 * (j // 2))
                        v.tensor_copy(stg[:, (j % 2) * P:(j % 2 + 1) * P],
                                      ps16h[j % 2][:, 0:P]).then_inc(*p.inc("dve"))

            @block.tensor
            def _(t):
                for tt in range(T // P):
                    t.wait_ge(p.hd("dve"), dve0 + tt * DVE_TT + NQ)
                    for c in range(IT22):
                        j = tt * IT22 + c
                        if j >= 2:
                            t.wait_ge(p.hd("dve"), dve0 + tt * DVE_TT + NQ + c - 1)
                        t.transpose(ps16h[j % 2][:, 0:P],
                                    hq16[:, (tt % 2) * 2816 + c * P:(tt % 2) * 2816 + (c + 1) * P],
                                    p.id16[:]).then_inc(*p.inc("pe"))

            @block.gpsimd
            def _(g):
                for tt in range(T // P):
                    for c in range(IT22):
                        j = tt * IT22 + c
                        g.wait_ge(p.hd("dve"), dve0 + tt * DVE_TT + NQ + c + 1)
                        g.dma_start(out=hqT[c * P:(c + 1) * P, tt * P:(tt + 1) * P],
                                    in_=stg[:, (j % 2) * P:(j % 2 + 1) * P]
                                    ).then_inc(*p.inc(ST[j % 2], 16))
                g.wait_ge(p.hd("sta"), p.cur("sta"))
                g.wait_ge(p.hd("stb"), p.cur("stb"))

    TG = 1024
    for tg in range(T // TG):
        with contextlib.ExitStack() as es:
            hh = p.sb([P, IT * TG], BF16, es)
            rin = p.sb([P, KT_D * TG], F32, es)
            ob = p.sb([P, 2 * 512], F32, es)
            with nc.Block() as block:
                @block.sync
                def _(sync):
                    for k in range(IT):
                        sync.dma_start(out=hh[:, k * TG:(k + 1) * TG],
                                       in_=hqT[k * P:(k + 1) * P, tg * TG:(tg + 1) * TG]
                                       ).then_inc(*p.inc("ld", 16))
                    for k in range(KT_D):
                        sync.dma_start(out=rin[:, k * TG:(k + 1) * TG],
                                       in_=r_in[k * P:(k + 1) * P, tg * TG:(tg + 1) * TG]
                                       ).then_inc(*p.inc("ld", 16))
                    sync.wait_ge(p.hd("ld"), p.cur("ld"))

            hh_t = [hh[:, k * TG:(k + 1) * TG] for k in range(IT)]
            evD0 = p.cur("evac")
            STp = ["sta", "stb"]
            stD0 = [p.cur("sta"), p.cur("stb")]

            def evD(kind, eng, it):
                i, o, tc = it["i"], it["o"], it["tc"]
                sl = slice((i % 2) * 512, (i % 2 + 1) * 512)
                if kind == "vector":
                    if i >= 2:
                        eng.wait_ge(p.hd(STp[i % 2]), stD0[i % 2] + 16 * (i // 2))
                    return eng.scalar_tensor_tensor(
                        ob[:, sl], it["psums"][0][:], 0.5,
                        rin[:, o * TG + tc * 512:o * TG + (tc + 1) * 512],
                        ALU.mult, ALU.add)
                if kind == "gpsimd":
                    eng.wait_ge(p.hd("evac"), evD0 + i + 1)
                    eng.dma_start(
                        out=r_out[o * P:(o + 1) * P,
                                  tg * TG + tc * 512:tg * TG + (tc + 1) * 512],
                        in_=ob[:, sl]).then_inc(*p.inc(STp[i % 2], 16))
                return None

            mm_stage(p, [wdq], I, D, hh_t, TG, evD, evac_done="vector",
                     use_gpsimd=True, o_group=128)
    return preq


def proj_cm(p, w16, act_tiles, evac, evac_done="vector", use_scalar=False,
            use_gpsimd=False, Tt=T):
    nc = p.nc
    ntc = Tt // 512
    n_iter = KT_D * ntc
    with contextlib.ExitStack() as es:
        wb = p.sb([P, KT_D * D], BF16, es)
        psums = [p.ps([P, 512], F32, es) for _ in range(2)]
        pe0, ev0 = p.cur("pe"), p.cur("evac")
        ld0 = p.cur("ld")
        with nc.Block() as block:
            @block.sync
            def _(sync):
                for k in range(KT_D):
                    sync.dma_start(out=wb[:, k * D:(k + 1) * D],
                                   in_=w16[k * P:(k + 1) * P, :]
                                   ).then_inc(*p.inc("ld", 16))
                sync.wait_ge(p.hd("ld"), p.cur("ld"))

            @block.tensor
            def _(t):
                t.wait_ge(p.hd("ld"), ld0 + 16 * KT_D)
                for i in range(n_iter):
                    o, tc = divmod(i, ntc)
                    if i >= 2:
                        t.wait_ge(p.hd("evac"), ev0 + i - 1)
                    for k in range(KT_D):
                        mm = t.matmul(psums[i % 2][:],
                                      wb[:, k * D + o * P:k * D + (o + 1) * P],
                                      act_tiles[k][:, tc * 512:(tc + 1) * 512],
                                      start=(k == 0), stop=(k == KT_D - 1))
                    mm.then_inc(*p.inc("pe"))

            if use_scalar:
                @block.scalar
                def _(sc):
                    for i in range(n_iter):
                        o, tc = divmod(i, ntc)
                        sc.wait_ge(p.hd("pe"), pe0 + i + 1)
                        r = evac("scalar", sc, dict(i=i, o=o, tc=tc,
                                                    psum=psums[i % 2]))
                        if evac_done == "scalar":
                            r.then_inc(*p.inc("evac"))

            @block.vector
            def _(v):
                for i in range(n_iter):
                    o, tc = divmod(i, ntc)
                    v.wait_ge(p.hd("pe"), pe0 + i + 1)
                    r = evac("vector", v, dict(i=i, o=o, tc=tc,
                                               psum=psums[i % 2]))
                    if evac_done == "vector":
                        r.then_inc(*p.inc("evac"))

            if use_gpsimd:
                @block.gpsimd
                def _(g):
                    for i in range(n_iter):
                        o, tc = divmod(i, ntc)
                        evac("gpsimd", g, dict(i=i, o=o, tc=tc, psum=None))
                    g.wait_ge(p.hd("sta"), p.cur("sta"))
                    g.wait_ge(p.hd("stb"), p.cur("stb"))


def mk_elu1_evac(p, tmp, dst_writer):
    dve0 = p.cur("dve")
    act0 = p.cur("act")
    ev0 = p.cur("evac")

    def evac(kind, eng, it):
        i = it["i"]
        sl = slice((i % 2) * 512, (i % 2 + 1) * 512)
        if kind == "scalar":
            eng.wait_ge(p.hd("dve"), dve0 + i + 1)
            eng.activation(tmp[:, sl], tmp[:, sl], AF.Exp
                           ).then_inc(*p.inc("act"))
            return None
        if kind == "vector":
            if i >= 2:
                eng.wait_ge(p.hd("evac"), ev0 + i - 1)
            eng.drain()
            eng.tensor_scalar_min(tmp[:, sl], it["psum"][:], 0.0
                                  ).then_inc(*p.inc("dve"))
            eng.wait_ge(p.hd("act"), act0 + i + 1)
            eng.scalar_tensor_tensor(tmp[:, sl], it["psum"][:], 0.0,
                                     tmp[:, sl], ALU.max, ALU.add)
            eng.drain()
            return dst_writer(eng, it, tmp[:, sl])
        return None
    return evac


def emit_l0_attn_p1(p, r1, qt16d, numloc, stot, wd):
    nc = p.nc
    NH = 16
    a16d = p.scratch(f"l0a16_{p.uid}", [D, T], BF16)
    ln_phase(p, r1, T, out16=a16d)
    wq16 = prep_weight(p, wd["WqT"], D, D)
    wk16 = prep_weight(p, wd["WkT"], D, D)
    wv16 = prep_weight(p, wd["WvT"], D, D)
    kt16d = p.scratch(f"l0kt_{p.uid}", [D, T], BF16)
    ktokd = p.scratch(f"l0ktok_{p.uid}", [T, D], BF16)
    vtokd = p.scratch(f"l0vtok_{p.uid}", [T, D], BF16)

    with contextlib.ExitStack() as es:
        a_sb = p.sb([P, KT_D * T], BF16, es)
        with nc.Block() as block:
            @block.sync
            def _(sync):
                for k in range(KT_D):
                    sync.dma_start(out=a_sb[:, k * T:(k + 1) * T],
                                   in_=a16d[k * P:(k + 1) * P, :]
                                   ).then_inc(*p.inc("ld", 16))
                sync.wait_ge(p.hd("ld"), p.cur("ld"))
        a_t = [a_sb[:, k * T:(k + 1) * T] for k in range(KT_D)]

        for (w16, dstd) in [(wq16, qt16d), (wk16, kt16d)]:
            with contextlib.ExitStack() as es2:
                tmp = p.sb([P, 2 * 512], F32, es2)
                stg = p.sb([P, 2 * 512], BF16, es2)
                ST = ["sta", "stb"]
                st_p = [p.cur("sta"), p.cur("stb")]
                ev0 = p.cur("evac")

                def wr(v, it, res):
                    i = it["i"]
                    sl = slice((i % 2) * 512, (i % 2 + 1) * 512)
                    if i >= 2:
                        v.wait_ge(p.hd(ST[i % 2]), st_p[i % 2] + 16 * (i // 2))
                    return v.tensor_copy(stg[:, sl], res)

                def gst(kind, eng, it):
                    i, o, tc = it["i"], it["o"], it["tc"]
                    sl = slice((i % 2) * 512, (i % 2 + 1) * 512)
                    eng.wait_ge(p.hd("evac"), ev0 + i + 1)
                    eng.dma_start(out=dstd[o * P:(o + 1) * P,
                                           tc * 512:(tc + 1) * 512],
                                  in_=stg[:, sl]).then_inc(*p.inc(ST[i % 2], 16))

                base = mk_elu1_evac(p, tmp, wr)

                def evac(kind, eng, it):
                    if kind == "gpsimd":
                        return gst(kind, eng, it)
                    return base(kind, eng, it)

                proj_cm(p, w16, a_t, evac, evac_done="vector",
                        use_scalar=True, use_gpsimd=True)

        with contextlib.ExitStack() as es2:
            wvb = p.sb([P, KT_D * D], BF16, es2)
            wkb = p.sb([P, KT_D * D], BF16, es2)
            tmp = p.sb([P, 2 * 512], F32, es2)
            stg = p.sb([P, 4 * 512], BF16, es2)
            psv = [p.ps([P, 512], F32, es2) for _ in range(2)]
            psk = [p.ps([P, 512], F32, es2) for _ in range(2)]
            ld0 = p.cur("ld")
            pe0, dve0, act0, ev0 = (p.cur("pe"), p.cur("dve"), p.cur("act"),
                                    p.cur("evac"))
            ST = ["sta", "stb"]
            st_p = [p.cur("sta"), p.cur("stb")]
            n_iter = (T // P) * 2
            with nc.Block() as block:
                @block.sync
                def _(sync):
                    for k in range(KT_D):
                        sync.dma_start(out=wvb[:, k * D:(k + 1) * D],
                                       in_=wv16[k * P:(k + 1) * P, :]
                                       ).then_inc(*p.inc("ld", 16))
                        sync.dma_start(out=wkb[:, k * D:(k + 1) * D],
                                       in_=wk16[k * P:(k + 1) * P, :]
                                       ).then_inc(*p.inc("ld", 16))
                    sync.wait_ge(p.hd("ld"), p.cur("ld"))

                @block.tensor
                def _(t):
                    t.wait_ge(p.hd("ld"), ld0 + 16 * 2 * KT_D)
                    for i in range(n_iter):
                        tt, half = divmod(i, 2)
                        osl = slice(half * 512, (half + 1) * 512)
                        if i >= 2:
                            t.wait_ge(p.hd("evac"), ev0 + 2 * (i - 2) + 2)
                        for k in range(KT_D):
                            t.matmul(psv[i % 2][:],
                                     a_t[k][:, tt * P:(tt + 1) * P],
                                     wvb[:, k * D + half * 512:k * D + (half + 1) * 512],
                                     start=(k == 0), stop=(k == KT_D - 1))
                        for k in range(KT_D):
                            mm = t.matmul(psk[i % 2][:],
                                          a_t[k][:, tt * P:(tt + 1) * P],
                                          wkb[:, k * D + half * 512:k * D + (half + 1) * 512],
                                          start=(k == 0), stop=(k == KT_D - 1))
                        mm.then_inc(*p.inc("pe"))

                @block.scalar
                def _(sc):
                    for i in range(n_iter):
                        sc.wait_ge(p.hd("dve"), dve0 + 3 * i + 1)
                        sl = slice((i % 2) * 512, (i % 2 + 1) * 512)
                        sc.activation(tmp[:, sl], tmp[:, sl], AF.Exp
                                      ).then_inc(*p.inc("act"))

                @block.vector
                def _(v):
                    for i in range(n_iter):
                        v.wait_ge(p.hd("pe"), pe0 + i + 1)
                        sl = slice((i % 2) * 512, (i % 2 + 1) * 512)
                        vs = slice((i % 2) * 512, (i % 2 + 1) * 512)
                        ks = slice(1024 + (i % 2) * 512, 1024 + (i % 2 + 1) * 512)
                        if i >= 2:
                            v.wait_ge(p.hd(ST[i % 2]), st_p[i % 2] + 32 * (i // 2))
                        v.tensor_copy(stg[:, vs], psv[i % 2][:])
                        v.tensor_scalar_min(tmp[:, sl], psk[i % 2][:], 0.0
                                            ).then_inc(*p.inc("dve", 3))
                        v.wait_ge(p.hd("act"), act0 + i + 1)
                        v.scalar_tensor_tensor(tmp[:, sl], psk[i % 2][:], 0.0,
                                               tmp[:, sl], ALU.max, ALU.add)
                        v.drain()
                        v.tensor_copy(stg[:, ks], tmp[:, sl]
                                      ).then_inc(*p.inc("evac", 2))

                @block.gpsimd
                def _(g):
                    for i in range(n_iter):
                        tt, half = divmod(i, 2)
                        vs = slice((i % 2) * 512, (i % 2 + 1) * 512)
                        ks = slice(1024 + (i % 2) * 512, 1024 + (i % 2 + 1) * 512)
                        g.wait_ge(p.hd("evac"), ev0 + 2 * i + 2)
                        g.dma_start(out=vtokd[tt * P:(tt + 1) * P,
                                              half * 512:(half + 1) * 512],
                                    in_=stg[:, vs]).then_inc(*p.inc(ST[i % 2], 16))
                        g.dma_start(out=ktokd[tt * P:(tt + 1) * P,
                                              half * 512:(half + 1) * 512],
                                    in_=stg[:, ks]).then_inc(*p.inc(ST[i % 2], 16))
                    g.wait_ge(p.hd("sta"), p.cur("sta"))
                    g.wait_ge(p.hd("stb"), p.cur("stb"))

    NCH = T // P
    with contextlib.ExitStack() as es:
        ktok2 = p.sb([P, NCH * P], BF16, es)
        v2 = p.sb([P, NCH * P], BF16, es)
        kT2 = p.sb([P, T], BF16, es)
        qT2 = p.sb([P, T], BF16, es)
        sloc = p.sb([P, NCH * 65], BF16, es)
        mask = p.sb([P, P], BF16, es)
        smT = [p.sb([P, P], BF16, es) for _ in range(2)]
        nl = p.sb([P, 4 * 65], F32, es)
        stsb = p.sb([64, 2 * 65], F32, es)
        scp = [p.ps([P, 512], F32, es) for _ in range(2)]
        ssp = [p.ps([64, 512], F32, es) for _ in range(2)]
        nmp = [p.ps([P, 512], F32, es) for _ in range(2)]

        with nc.Block() as block:
            @block.gpsimd
            def _(g):
                g.memset(mask[:], 1.0).then_inc(*p.inc("pool"))
                g.wait_ge(p.hd("pool"), p.cur("pool"))
                g.affine_select(mask[:], mask[:], [[1, P]], ALU.is_ge, 0.0,
                                base=0, channel_multiplier=-1)

        for hp in range(8):
            with nc.Block() as block:
                @block.sync
                def _(sync):
                    for tt in range(NCH):
                        sync.dma_start(out=ktok2[:, tt * P:(tt + 1) * P],
                                       in_=ktokd[tt * P:(tt + 1) * P,
                                                 hp * P:(hp + 1) * P]
                                       ).then_inc(*p.inc("ld", 16))
                        sync.dma_start(out=v2[:, tt * P:(tt + 1) * P],
                                       in_=vtokd[tt * P:(tt + 1) * P,
                                                 hp * P:(hp + 1) * P]
                                       ).then_inc(*p.inc("ld", 16))
                    sync.dma_start(out=kT2[:], in_=kt16d[hp * P:(hp + 1) * P, :]
                                   ).then_inc(*p.inc("ld", 16))
                    sync.dma_start(out=qT2[:], in_=qt16d[hp * P:(hp + 1) * P, :]
                                   ).then_inc(*p.inc("ld", 16))
                    sync.wait_ge(p.hd("ld"), p.cur("ld"))

            pe0, dve0 = p.cur("pe"), p.cur("dve")
            ST = ["sta", "stb"]
            st_p = [p.cur("sta"), p.cur("stb")]
            NS = 2 * NCH

            def hh_of(s):
                return (s % 2) * 64

            def n_scT(s):
                return 2 * s + 1 if s < 2 else 3 * s - 1

            def n_Sd(s):
                return 2 * s + 2 if s < 2 else 3 * s

            def n_num(sp):
                return 3 * sp + 7

            with nc.Block() as block:
                @block.tensor
                def _(t):
                    for s in range(NS):
                        i, hh = s // 2, hh_of(s)
                        csl = slice(i * P, (i + 1) * P)
                        if s >= 2:
                            t.wait_ge(p.hd("dve"), dve0 + n_scT(s - 2))
                        t.matmul(scp[s % 2][:, 0:P],
                                 kT2[hh:hh + 64, csl], qT2[hh:hh + 64, csl],
                                 start=True, stop=True).then_inc(*p.inc("pe"))
                        if s >= 2:
                            t.wait_ge(p.hd("dve"), dve0 + n_Sd(s - 2))
                        t.matmul(ssp[s % 2][:, 0:64],
                                 ktok2[:, i * P + hh:i * P + hh + 64],
                                 v2[:, i * P + hh:i * P + hh + 64],
                                 start=(i == 0), stop=False)
                        t.matmul(ssp[s % 2][:, 64:65],
                                 ktok2[:, i * P + hh:i * P + hh + 64],
                                 p.ones_col16[:],
                                 start=False, stop=(i == NCH - 1)
                                 ).then_inc(*p.inc("pe"))
                        if s >= 2:
                            sp = s - 2
                            ip, hhp = sp // 2, hh_of(sp)
                            cslp = slice(ip * P, (ip + 1) * P)
                            t.wait_ge(p.hd("dve"),
                                      dve0 + max(n_scT(sp),
                                                 n_num(sp - 2) if sp >= 2 else 0))
                            first = True
                            if ip > 0:
                                t.matmul(nmp[sp % 2][:, 0:65],
                                         qT2[hhp:hhp + 64, cslp],
                                         sloc[hhp:hhp + 64, (ip - 1) * 65:ip * 65],
                                         start=True, stop=False)
                                first = False
                            t.matmul(nmp[sp % 2][:, 0:64],
                                     smT[sp % 2][:],
                                     v2[:, ip * P + hhp:ip * P + hhp + 64],
                                     start=first, stop=False)
                            t.matmul(nmp[sp % 2][:, 64:65],
                                     smT[sp % 2][:], p.ones_col16[:],
                                     start=False, stop=True
                                     ).then_inc(*p.inc("pe"))

                @block.vector
                def _(v):
                    for s in range(NS):
                        i, hh = s // 2, hh_of(s)
                        if s >= 2:
                            v.wait_ge(p.hd("pe"), pe0 + n_num(s - 2))
                        else:
                            v.wait_ge(p.hd("pe"), pe0 + n_scT(s))
                        v.tensor_tensor(smT[s % 2][:], scp[s % 2][:, 0:P],
                                        mask[:], ALU.mult).then_inc(*p.inc("dve"))
                        v.wait_ge(p.hd("pe"), pe0 + n_Sd(s))
                        v.tensor_copy(sloc[hh:hh + 64, i * 65:(i + 1) * 65],
                                      ssp[s % 2][:, 0:65]).then_inc(*p.inc("dve"))
                        if s >= 2:
                            sp = s - 2
                            v.wait_ge(p.hd("pe"), pe0 + n_num(sp))
                            if sp >= 4:
                                v.wait_ge(p.hd(ST[sp % 2]),
                                          st_p[sp % 2] + 16 * ((sp - 4) // 2 + 1))
                            v.tensor_copy(nl[:, (sp % 4) * 65:(sp % 4 + 1) * 65],
                                          nmp[sp % 2][:, 0:65]
                                          ).then_inc(*p.inc("dve"))

                @block.gpsimd
                def _(g):
                    for s in range(NS - 2):
                        i = s // 2
                        h = 2 * hp + (s % 2)
                        g.wait_ge(p.hd("dve"), dve0 + n_num(s))
                        g.dma_start(out=numloc[i * P:(i + 1) * P,
                                               h * 65:(h + 1) * 65],
                                    in_=nl[:, (s % 4) * 65:(s % 4 + 1) * 65]
                                    ).then_inc(*p.inc(ST[s % 2], 16))

            pe0b, dve0b = p.cur("pe"), p.cur("dve")
            with nc.Block() as block:
                @block.tensor
                def _(t):
                    for sp in range(NS - 2, NS):
                        ip, hhp = sp // 2, hh_of(sp)
                        cslp = slice(ip * P, (ip + 1) * P)
                        t.matmul(nmp[sp % 2][:, 0:65],
                                 qT2[hhp:hhp + 64, cslp],
                                 sloc[hhp:hhp + 64, (ip - 1) * 65:ip * 65],
                                 start=True, stop=False)
                        t.matmul(nmp[sp % 2][:, 0:64],
                                 smT[sp % 2][:],
                                 v2[:, ip * P + hhp:ip * P + hhp + 64],
                                 start=False, stop=False)
                        t.matmul(nmp[sp % 2][:, 64:65],
                                 smT[sp % 2][:], p.ones_col16[:],
                                 start=False, stop=True).then_inc(*p.inc("pe"))

                @block.vector
                def _(v):
                    for j, sp in enumerate(range(NS - 2, NS)):
                        v.wait_ge(p.hd("pe"), pe0b + j + 1)
                        v.tensor_copy(nl[:, (sp % 4) * 65:(sp % 4 + 1) * 65],
                                      nmp[sp % 2][:, 0:65]).then_inc(*p.inc("dve"))
                    v.tensor_copy(stsb[:, 0:65], ssp[0][:, 0:65])
                    v.tensor_copy(stsb[:, 65:130], ssp[1][:, 0:65]
                                  ).then_inc(*p.inc("dve"))

                @block.gpsimd
                def _(g):
                    g.wait_ge(p.hd("dve"), dve0b + 3)
                    for j, sp in enumerate(range(NS - 2, NS)):
                        ip = sp // 2
                        h = 2 * hp + (sp % 2)
                        g.dma_start(out=numloc[ip * P:(ip + 1) * P,
                                               h * 65:(h + 1) * 65],
                                    in_=nl[:, (sp % 4) * 65:(sp % 4 + 1) * 65]
                                    ).then_inc(*p.inc("st", 16))
                    g.dma_start(out=stot[:, (2 * hp) * 65:(2 * hp + 2) * 65],
                                in_=stsb[:, 0:130]).then_inc(*p.inc("st", 16))
                    g.wait_ge(p.hd("st"), p.cur("st"))
                    g.wait_ge(p.hd("sta"), p.cur("sta"))
                    g.wait_ge(p.hd("stb"), p.cur("stb"))


def emit_l0_attn_p2(p, r1, r2, numloc, qt16d, stot_nbr, iso, wd):
    nc = p.nc
    wo16 = prep_weight(p, wd["WoT"], D, D)
    attn16d = p.scratch(f"l0at_{p.uid}", [T, D], BF16)
    atT16d = p.scratch(f"l0atT_{p.uid}", [D, T], BF16)

    with contextlib.ExitStack() as es:
        iso_sb = p.sb([1, 1], F32, es)
        iso_ps = p.ps([P, 512], F32, es)
        iso_col = p.sb([P, 1], F32, es)
        st32 = p.sb([64, 1040], F32, es)
        st16 = p.sb([P, 1040], BF16, es)
        with nc.Block() as block:
            @block.sync
            def _(sync):
                sync.dma_start(out=iso_sb[:], in_=iso[:]).then_inc(*p.inc("ld", 16))
                sync.dma_start(out=st32[:], in_=stot_nbr[:]).then_inc(*p.inc("ld", 16))
                sync.wait_ge(p.hd("ld"), p.cur("ld"))

            @block.tensor
            def _(t):
                t.wait_ge(p.hd("ld"), p.cur("ld"))
                t.matmul(iso_ps[:, 0:1], p.ones_row32[:], iso_sb[:],
                         start=True, stop=True).then_inc(*p.inc("pe"))

            @block.vector
            def _(v):
                v.wait_ge(p.hd("pe"), p.cur("pe"))
                v.tensor_copy(iso_col[:], iso_ps[:, 0:1])
                v.drain()
                v.tensor_scalar_mul(st32[:], st32[:], iso_col[0:64, :])
                v.drain()
                v.tensor_copy(st16[0:64, :], st32[:])
                v.tensor_copy(st16[64:128, :], st32[:])

        with contextlib.ExitStack() as es2:
            qT2 = p.sb([P, T], BF16, es2)
            nlb = p.sb([P, 4 * 130], F32, es2)
            tb = p.sb([P, 2 * 65], F32, es2)
            rcol = p.sb([P, 2], F32, es2)
            atn = p.sb([P, 2 * P], BF16, es2)
            nip = [p.ps([P, 512], F32, es2) for _ in range(2)]
            NCH = T // P
            for hp in range(8):
                with nc.Block() as block:
                    @block.sync
                    def _(sync):
                        sync.dma_start(out=qT2[:],
                                       in_=qt16d[hp * P:(hp + 1) * P, :]
                                       ).then_inc(*p.inc("ld", 16))
                        sync.wait_ge(p.hd("ld"), p.cur("ld"))

                ld0 = p.cur("lda")
                ld_p = [p.cur("lda"), p.cur("ldb")]
                LD = ["lda", "ldb"]
                ST = ["sta", "stb"]
                st_p = [p.cur("sta"), p.cur("stb")]
                pe0, dve0 = p.cur("pe"), p.cur("dve")
                with nc.Block() as block:
                    @block.sync
                    def _(sync):
                        for i in range(NCH):
                            if i >= 4:
                                sync.wait_ge(p.hd("dve"), dve0 + 8 * (i - 4) + 8)
                            sync.dma_start(
                                out=nlb[:, (i % 4) * 130:(i % 4 + 1) * 130],
                                in_=numloc[i * P:(i + 1) * P,
                                           (2 * hp) * 65:(2 * hp + 2) * 65]
                            ).then_inc(*p.inc(LD[i % 2], 16))
                        sync.wait_ge(p.hd("lda"), p.cur("lda"))
                        sync.wait_ge(p.hd("ldb"), p.cur("ldb"))

                    @block.tensor
                    def _(t):
                        for i in range(NCH):
                            t.wait_ge(p.hd(LD[i % 2]),
                                      ld_p[i % 2] + 16 * (i // 2 + 1))
                            for s in range(2):
                                j = 2 * i + s
                                if j >= 2:
                                    t.wait_ge(p.hd("dve"), dve0 + 4 * (j - 2) + 1)
                                h = 2 * hp + s
                                t.matmul(nip[j % 2][:, 0:65],
                                         qT2[s * 64:s * 64 + 64, i * P:(i + 1) * P],
                                         st16[s * 64:s * 64 + 64, h * 65:(h + 1) * 65],
                                         start=True, stop=True
                                         ).then_inc(*p.inc("pe"))

                    @block.vector
                    def _(v):
                        for i in range(NCH):
                            for s in range(2):
                                j = 2 * i + s
                                v.wait_ge(p.hd("pe"), pe0 + j + 1)
                                if i >= 2 and s == 0:
                                    v.wait_ge(p.hd(ST[i % 2]),
                                              st_p[i % 2] + 16 * (i // 2))
                                v.drain()
                                v.tensor_tensor(tb[:, s * 65:(s + 1) * 65],
                                                nip[j % 2][:, 0:65],
                                                nlb[:, (i % 4) * 130 + s * 65:
                                                    (i % 4) * 130 + (s + 1) * 65],
                                                ALU.add).then_inc(*p.inc("dve"))
                                v.drain()
                                v.reciprocal(rcol[:, s:s + 1],
                                             tb[:, s * 65 + 64:s * 65 + 65]
                                             ).then_inc(*p.inc("dve"))
                                v.drain()
                                v.tensor_scalar(atn[:, (i % 2) * P + s * 64:
                                                    (i % 2) * P + s * 64 + 64],
                                                tb[:, s * 65:s * 65 + 64],
                                                rcol[:, s:s + 1], 1e-12,
                                                ALU.mult, ALU.add
                                                ).then_inc(*p.inc("dve", 2))

                    @block.gpsimd
                    def _(g):
                        for i in range(NCH):
                            g.wait_ge(p.hd("dve"), dve0 + 8 * i + 8)
                            g.dma_start(out=attn16d[i * P:(i + 1) * P,
                                                    hp * P:(hp + 1) * P],
                                        in_=atn[:, (i % 2) * P:(i % 2 + 1) * P]
                                        ).then_inc(*p.inc(ST[i % 2], 16))
                        g.wait_ge(p.hd("sta"), p.cur("sta"))
                        g.wait_ge(p.hd("stb"), p.cur("stb"))

        with contextlib.ExitStack() as es2:
            ab = p.sb([P, 2 * D], BF16, es2)
            stg = p.sb([P, 2 * P], BF16, es2)
            ps16 = [p.ps([P, 1024], BF16, es2) for _ in range(2)]
            LD = ["lda", "ldb"]
            ld_p = [p.cur("lda"), p.cur("ldb")]
            ST = ["sta", "stb"]
            st_p = [p.cur("sta"), p.cur("stb")]
            pe0, dve0 = p.cur("pe"), p.cur("dve")
            with nc.Block() as block:
                @block.sync
                def _(sync):
                    for tt in range(T // P):
                        if tt >= 2:
                            sync.wait_ge(p.hd("pe"), pe0 + (tt - 1) * KT_D)
                        sync.dma_start(out=ab[:, (tt % 2) * D:(tt % 2 + 1) * D],
                                       in_=attn16d[tt * P:(tt + 1) * P, :]
                                       ).then_inc(*p.inc(LD[tt % 2], 16))
                    sync.wait_ge(p.hd("lda"), p.cur("lda"))
                    sync.wait_ge(p.hd("ldb"), p.cur("ldb"))

                @block.tensor
                def _(t):
                    for tt in range(T // P):
                        t.wait_ge(p.hd(LD[tt % 2]),
                                  ld_p[tt % 2] + 16 * (tt // 2 + 1))
                        for k in range(KT_D):
                            j = tt * KT_D + k
                            if j >= 2:
                                t.wait_ge(p.hd("dve"), dve0 + j - 1)
                            t.transpose(ps16[j % 2][:, 0:P],
                                        ab[:, (tt % 2) * D + k * P:(tt % 2) * D + (k + 1) * P],
                                        p.id16[:]).then_inc(*p.inc("pe"))

                @block.vector
                def _(v):
                    for tt in range(T // P):
                        for k in range(KT_D):
                            j = tt * KT_D + k
                            v.wait_ge(p.hd("pe"), pe0 + j + 1)
                            if j >= 2:
                                v.wait_ge(p.hd(ST[j % 2]),
                                          st_p[j % 2] + 16 * (j // 2))
                            v.tensor_copy(stg[:, (j % 2) * P:(j % 2 + 1) * P],
                                          ps16[j % 2][:, 0:P]
                                          ).then_inc(*p.inc("dve"))

                @block.gpsimd
                def _(g):
                    for tt in range(T // P):
                        for k in range(KT_D):
                            j = tt * KT_D + k
                            g.wait_ge(p.hd("dve"), dve0 + j + 1)
                            g.dma_start(out=atT16d[k * P:(k + 1) * P,
                                                   tt * P:(tt + 1) * P],
                                        in_=stg[:, (j % 2) * P:(j % 2 + 1) * P]
                                        ).then_inc(*p.inc(ST[j % 2], 16))
                    g.wait_ge(p.hd("sta"), p.cur("sta"))
                    g.wait_ge(p.hd("stb"), p.cur("stb"))

        TG = 1024
        for tg in range(T // TG):
            with contextlib.ExitStack() as es2:
                at = p.sb([P, KT_D * TG], BF16, es2)
                rin = p.sb([P, KT_D * TG], F32, es2)
                ob = p.sb([P, 2 * 512], F32, es2)
                with nc.Block() as block:
                    @block.sync
                    def _(sync):
                        for k in range(KT_D):
                            sync.dma_start(out=at[:, k * TG:(k + 1) * TG],
                                           in_=atT16d[k * P:(k + 1) * P,
                                                      tg * TG:(tg + 1) * TG]
                                           ).then_inc(*p.inc("ld", 16))
                            sync.dma_start(out=rin[:, k * TG:(k + 1) * TG],
                                           in_=r1[k * P:(k + 1) * P,
                                                  tg * TG:(tg + 1) * TG]
                                           ).then_inc(*p.inc("ld", 16))
                        sync.wait_ge(p.hd("ld"), p.cur("ld"))

                at_t = [at[:, k * TG:(k + 1) * TG] for k in range(KT_D)]
                evO0 = p.cur("evac")
                STp = ["sta", "stb"]
                stO0 = [p.cur("sta"), p.cur("stb")]

                def evO(kind, eng, it):
                    i, o, tc = it["i"], it["o"], it["tc"]
                    sl = slice((i % 2) * 512, (i % 2 + 1) * 512)
                    if kind == "vector":
                        if i >= 2:
                            eng.wait_ge(p.hd(STp[i % 2]), stO0[i % 2] + 16 * (i // 2))
                        return eng.scalar_tensor_tensor(
                            ob[:, sl], it["psums"][0][:], 1.0,
                            rin[:, o * TG + tc * 512:o * TG + (tc + 1) * 512],
                            ALU.mult, ALU.add)
                    if kind == "gpsimd":
                        eng.wait_ge(p.hd("evac"), evO0 + i + 1)
                        eng.dma_start(
                            out=r2[o * P:(o + 1) * P,
                                   tg * TG + tc * 512:tg * TG + (tc + 1) * 512],
                            in_=ob[:, sl]).then_inc(*p.inc(STp[i % 2], 16))
                    return None

                mm_stage(p, [wo16], D, D, at_t, TG, evO, evac_done="vector",
                         use_gpsimd=True, o_group=128)


def bias_row_cols(p, bcols, w32T, O):
    nc = p.nc
    p.uid += 1
    bo_d = p.scratch(f"bo_{p.uid}", [1, O], F32)
    ot = O // P
    cols = p.sb([P, ot], F32)
    with contextlib.ExitStack() as es:
        b32 = p.sb([P, KT_D], F32, es)
        b16 = p.sb([P, KT_D], BF16, es)
        row = p.sb([1, O], F32, es)
        pso = p.ps([1, 512], F32, es)
        wf = p.sb([P, O], F32, es)
        wb = p.sb([P, KT_D * O], BF16, es)
        with nc.Block() as block:
            @block.scalar
            def _(sc):
                sc.dma_start(out=b32[:], in_=bcols[:]).then_inc(*p.inc("ld2", 16))
                sc.wait_ge(p.hd("ld2"), p.cur("ld2"))

            @block.vector
            def _(v):
                v.wait_ge(p.hd("ld2"), p.cur("ld2"))
                v.tensor_copy(b16[:], b32[:])

        for k in range(KT_D):
            with nc.Block() as block:
                @block.sync
                def _(sync):
                    sync.dma_start(out=wf[:], in_=w32T[k * P:(k + 1) * P, :]
                                   ).then_inc(*p.inc("ld", 16))
                    sync.wait_ge(p.hd("ld"), p.cur("ld"))

                @block.vector
                def _(v):
                    v.wait_ge(p.hd("ld"), p.cur("ld"))
                    v.tensor_copy(wb[:, k * O:(k + 1) * O], wf[:])

        n_ch = O // 512
        pe0, dve0 = p.cur("pe"), p.cur("dve")
        with nc.Block() as block:
            @block.tensor
            def _(t):
                for c in range(n_ch):
                    if c >= 1:
                        t.wait_ge(p.hd("dve"), dve0 + c)
                    for k in range(KT_D):
                        mm = t.matmul(pso[:, 0:512], b16[:, k:k + 1],
                                      wb[:, k * O + c * 512:k * O + (c + 1) * 512],
                                      start=(k == 0), stop=(k == KT_D - 1))
                    mm.then_inc(*p.inc("pe"))

            @block.vector
            def _(v):
                for c in range(n_ch):
                    v.wait_ge(p.hd("pe"), pe0 + c + 1)
                    v.tensor_copy(row[:, c * 512:(c + 1) * 512], pso[:, 0:512]
                                  ).then_inc(*p.inc("dve"))

            @block.gpsimd
            def _(g):
                g.wait_ge(p.hd("dve"), dve0 + n_ch)
                g.dma_start(out=bo_d[:], in_=row[:]).then_inc(*p.inc("st", 16))
                g.wait_ge(p.hd("st"), p.cur("st"))

        with nc.Block() as block:
            @block.scalar
            def _(sc):
                sc.dma_start(out=cols[:],
                             in_=bo_d[:].rearrange("o (a b) -> (o b) a", b=P),
                             allow_slow_non_contiguous=True
                             ).then_inc(*p.inc("ld2", 16))
                sc.wait_ge(p.hd("ld2"), p.cur("ld2"))
    return cols


def emit_l1_attn_p1(p, r4, a32, numl, denl, sigr, carry, iso, wd):
    nc = p.nc
    p.constcol(-1.0)
    ln_phase(p, r4, T, out32=a32)
    l16d = p.scratch(f"l1l16_{p.uid}", [D, T], BF16)
    ln_phase(p, a32, T, out16=l16d)
    wk16 = prep_weight(p, wd["WkT"], D, D, row_scale=wd["g1c"])
    wv16 = prep_weight(p, wd["WvT"], D, D, row_scale=wd["g1c"])
    wr16 = prep_weight(p, wd["WrT"], D, D, row_scale=wd["g1c"])
    bok = bias_row_cols(p, wd["b1c"], wd["WkT"], D)
    bov = bias_row_cols(p, wd["b1c"], wd["WvT"], D)
    bor = bias_row_cols(p, wd["b1c"], wd["WrT"], D)

    with contextlib.ExitStack() as es:
        l_sb = p.sb([P, KT_D * T], BF16, es)
        iso_sb = p.sb([1, 1], F32, es)
        iso_ps = p.ps([P, 512], F32, es)
        omi = p.sb([P, 1], F32, es)
        tdc = p.sb([P, KT_D], F32, es)
        tfc = p.sb([P, KT_D], F32, es)
        with nc.Block() as block:
            @block.sync
            def _(sync):
                for k in range(KT_D):
                    sync.dma_start(out=l_sb[:, k * T:(k + 1) * T],
                                   in_=l16d[k * P:(k + 1) * P, :]
                                   ).then_inc(*p.inc("ld", 16))
                sync.dma_start(out=iso_sb[:], in_=iso[:]).then_inc(*p.inc("ld", 16))
                sync.dma_start(out=tdc[:], in_=wd["tdc"]).then_inc(*p.inc("ld", 16))
                sync.dma_start(out=tfc[:], in_=wd["tfc"]).then_inc(*p.inc("ld", 16))
                sync.wait_ge(p.hd("ld"), p.cur("ld"))

            @block.tensor
            def _(t):
                t.wait_ge(p.hd("ld"), p.cur("ld"))
                t.matmul(iso_ps[:, 0:1], p.ones_row32[:], iso_sb[:],
                         start=True, stop=True).then_inc(*p.inc("pe"))

            @block.vector
            def _(v):
                v.wait_ge(p.hd("pe"), p.cur("pe"))
                v.tensor_scalar(omi[:], iso_ps[:, 0:1], -1.0, 1.0,
                                ALU.mult, ALU.add)
        l_t = [l_sb[:, k * T:(k + 1) * T] for k in range(KT_D)]

        wkb = p.sb([P, KT_D * D], BF16, es)
        wvb = p.sb([P, KT_D * D], BF16, es)
        wrb = p.sb([P, KT_D * D], BF16, es)
        with nc.Block() as block:
            @block.sync
            def _(sync):
                for k in range(KT_D):
                    for (wb, w16) in [(wkb, wk16), (wvb, wv16), (wrb, wr16)]:
                        sync.dma_start(out=wb[:, k * D:(k + 1) * D],
                                       in_=w16[k * P:(k + 1) * P, :]
                                       ).then_inc(*p.inc("ld", 16))
                sync.wait_ge(p.hd("ld"), p.cur("ld"))

        for ct in range(KT_D):
            with contextlib.ExitStack() as es2:
                k32 = p.sb([P, T], F32, es2)
                v32 = p.sb([P, T], F32, es2)
                sg32 = p.sb([P, T], F32, es2)
                w32 = p.sb([P, T], F32, es2)
                dbuf = p.sb([P, T], F32, es2)
                nbuf = k32
                cdat = p.sb([P, 8], F32, es2)
                psk = [p.ps([P, 512], F32, es2) for _ in range(2)]
                pe0, dve0, act0 = p.cur("pe"), p.cur("dve"), p.cur("act")
                with nc.Block() as block:
                    @block.tensor
                    def _(t):
                        for i in range(3 * (T // 512)):
                            w_i, tc = divmod(i, T // 512)
                            wb = [wkb, wvb, wrb][w_i]
                            if i >= 2:
                                t.wait_ge(p.hd("dve"), dve0 + i - 1)
                            for k in range(KT_D):
                                mm = t.matmul(psk[i % 2][:],
                                              wb[:, k * D + ct * P:k * D + (ct + 1) * P],
                                              l_t[k][:, tc * 512:(tc + 1) * 512],
                                              start=(k == 0), stop=(k == KT_D - 1))
                            mm.then_inc(*p.inc("pe"))

                    @block.scalar
                    def _(sc):
                        for i in range(3 * (T // 512)):
                            w_i, tc = divmod(i, T // 512)
                            if w_i != 2:
                                continue
                            sc.wait_ge(p.hd("pe"), pe0 + i + 1)
                            sc.activation(sg32[:, tc * 512:(tc + 1) * 512],
                                          psk[i % 2][:], AF.Sigmoid,
                                          bias=bor[:, ct:ct + 1]
                                          ).then_inc(*p.inc("act"))

                    @block.vector
                    def _(v):
                        for i in range(3 * (T // 512)):
                            w_i, tc = divmod(i, T // 512)
                            if w_i == 2:
                                v.wait_ge(p.hd("act"),
                                          act0 + (i - 2 * (T // 512)) + 1)
                                v.sem_inc(p.hd("dve"), 1)
                                p.c["dve"] += 1
                                continue
                            v.wait_ge(p.hd("pe"), pe0 + i + 1)
                            dst = k32 if w_i == 0 else v32
                            bo = bok if w_i == 0 else bov
                            v.tensor_scalar_add(dst[:, tc * 512:(tc + 1) * 512],
                                                psk[i % 2][:], bo[:, ct:ct + 1]
                                                ).then_inc(*p.inc("dve"))

                with nc.Block() as block:
                    @block.scalar
                    def _(sc):
                        sc.activation(cdat[:, 2:3], tdc[:, ct:ct + 1], AF.Exp)
                        sc.drain()
                        sc.activation(cdat[:, 0:1], cdat[:, 2:3], AF.Exp,
                                      scale=mone_col(p)[:])
                        sc.drain()
                        sc.activation(cdat[:, 1:2], tfc[:, ct:ct + 1], AF.Exp,
                                      scale=omi[:]).then_inc(*p.inc("act"))

                    @block.vector
                    def _(v):
                        v.tensor_scalar_min(w32[:], k32[:], 30.0
                                            ).then_inc(*p.inc("dve"))

                with nc.Block() as block:
                    @block.scalar
                    def _(sc):
                        sc.activation(w32[:, 0:2048], w32[:, 0:2048], AF.Exp)
                        sc.activation(w32[:, 2048:T], w32[:, 2048:T], AF.Exp
                                      ).then_inc(*p.inc("act"))

                with nc.Block() as block:
                    @block.vector
                    def _(v):
                        v.tensor_tensor(v32[:], w32[:], v32[:], ALU.mult)
                        v.drain()
                        v.tensor_scalar_mul(v32[:, 0:1], v32[:, 0:1],
                                            cdat[:, 1:2])
                        v.tensor_scalar_mul(w32[:, 0:1], w32[:, 0:1],
                                            cdat[:, 1:2])
                        v.drain()
                        db = cdat[:, 0:1].broadcast_to([P, T])
                        v.tensor_tensor_scan(nbuf[:], db, v32[:], 0.0,
                                             ALU.mult, ALU.add)
                        v.drain()
                        v.tensor_tensor_scan(dbuf[:], db, w32[:], 0.0,
                                             ALU.mult, ALU.add)
                        v.drain()
                        v.tensor_copy(cdat[:, 4:5], nbuf[:, T - 1:T])
                        v.tensor_copy(cdat[:, 5:6], dbuf[:, T - 1:T]
                                      ).then_inc(*p.inc("dve"))

                    @block.gpsimd
                    def _(g):
                        g.wait_ge(p.hd("dve"), p.cur("dve"))
                        g.dma_start(out=numl[ct * P:(ct + 1) * P, :],
                                    in_=nbuf[:]).then_inc(*p.inc("st", 16))
                        g.dma_start(out=denl[ct * P:(ct + 1) * P, :],
                                    in_=dbuf[:]).then_inc(*p.inc("st", 16))
                        g.dma_start(out=sigr[ct * P:(ct + 1) * P, :],
                                    in_=sg32[:]).then_inc(*p.inc("st", 16))
                        g.dma_start(out=carry[ct * P:(ct + 1) * P, :],
                                    in_=cdat[:, 4:6]).then_inc(*p.inc("st", 16))
                        g.wait_ge(p.hd("st"), p.cur("st"))


def mone_col(p):
    return p.constcol(-1.0)


def emit_l1_attn_p2(p, r4, r5, a32, numl, denl, sigr, carry_nbr, iso, wd):
    nc = p.nc
    p.constcol(-1.0)
    wo16 = prep_weight(p, wd["WoT"], D, D)
    wffk16 = prep_weight(p, wd["WffkT"], D, 4096, row_scale=wd["g2c"])
    wffv16 = prep_weight(p, wd["WffvT"], 4096, D)
    boffk = bias_row_cols(p, wd["b2c"], wd["WffkT"], 4096)
    rwkv16d = p.scratch(f"l1rw_{p.uid}", [D, T], BF16)
    b32d = p.scratch(f"l1b_{p.uid}", [D, T], F32)

    with contextlib.ExitStack() as es:
        iso_sb = p.sb([1, 1], F32, es)
        iso_ps = p.ps([P, 512], F32, es)
        iso_col = p.sb([P, 1], F32, es)
        tdc = p.sb([P, KT_D], F32, es)
        iota = p.sb([P, T], F32, es)
        with nc.Block() as block:
            @block.sync
            def _(sync):
                sync.dma_start(out=iso_sb[:], in_=iso[:]).then_inc(*p.inc("ld", 16))
                sync.dma_start(out=tdc[:], in_=wd["tdc"]).then_inc(*p.inc("ld", 16))
                sync.wait_ge(p.hd("ld"), p.cur("ld"))

            @block.gpsimd
            def _(g):
                g.iota(iota[:], [[1, T]], base=0, channel_multiplier=0,
                       allow_small_or_imprecise_dtypes=True)

            @block.tensor
            def _(t):
                t.wait_ge(p.hd("ld"), p.cur("ld"))
                t.matmul(iso_ps[:, 0:1], p.ones_row32[:], iso_sb[:],
                         start=True, stop=True).then_inc(*p.inc("pe"))

            @block.vector
            def _(v):
                v.wait_ge(p.hd("pe"), p.cur("pe"))
                v.tensor_copy(iso_col[:], iso_ps[:, 0:1])

        for ct in range(KT_D):
            with contextlib.ExitStack() as es2:
                nb = p.sb([P, T], F32, es2)
                db = p.sb([P, T], F32, es2)
                sg = p.sb([P, T], F32, es2)
                dpow = p.sb([P, T], F32, es2)
                rcb = p.sb([P, T], F32, es2)
                rw16 = p.sb([P, T], BF16, es2)
                cc = p.sb([P, 8], F32, es2)
                with nc.Block() as block:
                    @block.sync
                    def _(sync):
                        sync.dma_start(out=nb[:], in_=numl[ct * P:(ct + 1) * P, :]
                                       ).then_inc(*p.inc("ld", 16))
                        sync.dma_start(out=db[:], in_=denl[ct * P:(ct + 1) * P, :]
                                       ).then_inc(*p.inc("ld", 16))
                        sync.dma_start(out=sg[:], in_=sigr[ct * P:(ct + 1) * P, :]
                                       ).then_inc(*p.inc("ld", 16))
                        sync.dma_start(out=cc[:, 0:2],
                                       in_=carry_nbr[ct * P:(ct + 1) * P, :]
                                       ).then_inc(*p.inc("ld", 16))
                        sync.wait_ge(p.hd("ld"), p.cur("ld"))

                with nc.Block() as block:
                    @block.scalar
                    def _(sc):
                        sc.activation(cc[:, 2:3], tdc[:, ct:ct + 1], AF.Exp)
                        sc.drain()
                        sc.activation(cc[:, 3:4], cc[:, 2:3], AF.Copy,
                                      scale=mone_col(p)[:])
                        sc.drain()
                        sc.activation(dpow[:], iota[:], AF.Exp,
                                      scale=cc[:, 3:4], bias=cc[:, 3:4]
                                      ).then_inc(*p.inc("act"))

                    @block.vector
                    def _(v):
                        v.tensor_scalar_mul(cc[:, 4:6], cc[:, 0:2], iso_col[:])

                with nc.Block() as block:
                    @block.vector
                    def _(v):
                        v.scalar_tensor_tensor(nb[:], dpow[:], cc[:, 4:5], nb[:],
                                               ALU.mult, ALU.add)
                        v.scalar_tensor_tensor(db[:], dpow[:], cc[:, 5:6], db[:],
                                               ALU.mult, ALU.add)
                        v.drain()
                        v.tensor_scalar_add(db[:], db[:], 1e-6)
                        v.drain().then_inc(*p.inc("dve"))

                    @block.scalar
                    def _(sc):
                        sc.wait_ge(p.hd("dve"), p.cur("dve"))
                        sc.activation(rcb[:, 0:2048], db[:, 0:2048], AF.Ln)
                        sc.activation(rcb[:, 2048:T], db[:, 2048:T], AF.Ln)
                        sc.drain()
                        sc.activation(rcb[:, 0:2048], rcb[:, 0:2048], AF.Exp,
                                      scale=mone_col(p)[:])
                        sc.activation(rcb[:, 2048:T], rcb[:, 2048:T], AF.Exp,
                                      scale=mone_col(p)[:]).then_inc(*p.inc("act"))

                with nc.Block() as block:
                    @block.vector
                    def _(v):
                        v.tensor_tensor(dpow[:], db[:], rcb[:], ALU.mult)
                        v.drain()
                        v.tensor_scalar(dpow[:], dpow[:], -1.0, 2.0,
                                        ALU.mult, ALU.add)
                        v.drain()
                        v.tensor_tensor(rcb[:], rcb[:], dpow[:], ALU.mult)
                        v.drain()
                        v.tensor_tensor(nb[:], nb[:], rcb[:], ALU.mult)
                        v.drain()
                        v.tensor_tensor(rw16[:], nb[:], sg[:], ALU.mult
                                        ).then_inc(*p.inc("dve"))

                    @block.gpsimd
                    def _(g):
                        g.wait_ge(p.hd("dve"), p.cur("dve"))
                        g.dma_start(out=rwkv16d[ct * P:(ct + 1) * P, :],
                                    in_=rw16[:]).then_inc(*p.inc("st", 16))
                        g.wait_ge(p.hd("st"), p.cur("st"))

        TG = 1024
        for tg in range(T // TG):
            with contextlib.ExitStack() as es2:
                rw = p.sb([P, KT_D * TG], BF16, es2)
                ain = p.sb([P, KT_D * TG], F32, es2)
                ob = p.sb([P, 2 * 512], F32, es2)
                with nc.Block() as block:
                    @block.sync
                    def _(sync):
                        for k in range(KT_D):
                            sync.dma_start(out=rw[:, k * TG:(k + 1) * TG],
                                           in_=rwkv16d[k * P:(k + 1) * P,
                                                       tg * TG:(tg + 1) * TG]
                                           ).then_inc(*p.inc("ld", 16))
                            sync.dma_start(out=ain[:, k * TG:(k + 1) * TG],
                                           in_=a32[k * P:(k + 1) * P,
                                                   tg * TG:(tg + 1) * TG]
                                           ).then_inc(*p.inc("ld", 16))
                        sync.wait_ge(p.hd("ld"), p.cur("ld"))

                rw_t = [rw[:, k * TG:(k + 1) * TG] for k in range(KT_D)]
                ev0 = p.cur("evac")
                STp = ["sta", "stb"]
                st0 = [p.cur("sta"), p.cur("stb")]

                def evB(kind, eng, it):
                    i, o, tc = it["i"], it["o"], it["tc"]
                    sl = slice((i % 2) * 512, (i % 2 + 1) * 512)
                    if kind == "vector":
                        if i >= 2:
                            eng.wait_ge(p.hd(STp[i % 2]), st0[i % 2] + 16 * (i // 2))
                        return eng.scalar_tensor_tensor(
                            ob[:, sl], it["psums"][0][:], 1.0,
                            ain[:, o * TG + tc * 512:o * TG + (tc + 1) * 512],
                            ALU.mult, ALU.add)
                    if kind == "gpsimd":
                        eng.wait_ge(p.hd("evac"), ev0 + i + 1)
                        eng.dma_start(
                            out=b32d[o * P:(o + 1) * P,
                                     tg * TG + tc * 512:tg * TG + (tc + 1) * 512],
                            in_=ob[:, sl]).then_inc(*p.inc(STp[i % 2], 16))
                    return None

                mm_stage(p, [wo16], D, D, rw_t, TG, evB, evac_done="vector",
                         use_gpsimd=True, o_group=128)

        xl2d = p.scratch(f"l1x2_{p.uid}", [D, T], BF16)
        ln_phase(p, b32d, T, out16=xl2d)
        H = 4096
        for tg in range(T // TG):
            with contextlib.ExitStack() as es2:
                x2 = p.sb([P, KT_D * TG], BF16, es2)
                hid = p.sb([P, (H // P) * TG], BF16, es2)
                t1 = p.sb([P, 2 * 512], F32, es2)
                with nc.Block() as block:
                    @block.sync
                    def _(sync):
                        for k in range(KT_D):
                            sync.dma_start(out=x2[:, k * TG:(k + 1) * TG],
                                           in_=xl2d[k * P:(k + 1) * P,
                                                    tg * TG:(tg + 1) * TG]
                                           ).then_inc(*p.inc("ld", 16))
                        sync.wait_ge(p.hd("ld"), p.cur("ld"))

                x2_t = [x2[:, k * TG:(k + 1) * TG] for k in range(KT_D)]
                hid_t = [hid[:, k * TG:(k + 1) * TG] for k in range(H // P)]
                act_marks = {}
                evK0 = p.cur("evac")

                def evK(kind, eng, it):
                    i, o, tc = it["i"], it["o"], it["tc"]
                    sl = slice((i % 2) * 512, (i % 2 + 1) * 512)
                    if kind == "scalar":
                        if i >= 2:
                            eng.wait_ge(p.hd("evac"), evK0 + i - 1)
                        eng.activation(t1[:, sl], it["psums"][0][:], AF.Relu,
                                       bias=boffk[:, o:o + 1]
                                       ).then_inc(*p.inc("act"))
                        act_marks[i] = p.cur("act")
                        return None
                    if kind == "vector":
                        eng.wait_ge(p.hd("act"), act_marks[i])
                        return eng.tensor_tensor(
                            hid[:, o * TG + tc * 512:o * TG + (tc + 1) * 512],
                            t1[:, sl], t1[:, sl], ALU.mult)
                    return None

                mm_stage(p, [wffk16], D, H, x2_t, TG, evK, evac_done="vector",
                         use_scalar=True, use_gpsimd=False, o_group=512)

                es3 = contextlib.ExitStack()
                bin_ = p.sb([P, KT_D * TG], F32, es3)
                rin = p.sb([P, KT_D * TG], F32, es3)
                ob = p.sb([P, 2 * 512], F32, es3)
                with nc.Block() as block:
                    @block.sync
                    def _(sync):
                        for k in range(KT_D):
                            sync.dma_start(out=bin_[:, k * TG:(k + 1) * TG],
                                           in_=b32d[k * P:(k + 1) * P,
                                                    tg * TG:(tg + 1) * TG]
                                           ).then_inc(*p.inc("ld", 16))
                            sync.dma_start(out=rin[:, k * TG:(k + 1) * TG],
                                           in_=r4[k * P:(k + 1) * P,
                                                  tg * TG:(tg + 1) * TG]
                                           ).then_inc(*p.inc("ld", 16))
                        sync.wait_ge(p.hd("ld"), p.cur("ld"))

                ev0 = p.cur("evac")
                STp = ["sta", "stb"]
                st0 = [p.cur("sta"), p.cur("stb")]

                def evV(kind, eng, it):
                    i, o, tc = it["i"], it["o"], it["tc"]
                    sl = slice((i % 2) * 512, (i % 2 + 1) * 512)
                    csl = slice(o * TG + tc * 512, o * TG + (tc + 1) * 512)
                    if kind == "vector":
                        if i >= 2:
                            eng.wait_ge(p.hd(STp[i % 2]), st0[i % 2] + 16 * (i // 2))
                        eng.tensor_tensor(ob[:, sl], it["psums"][0][:],
                                          bin_[:, csl], ALU.add)
                        eng.drain()
                        return eng.tensor_tensor(ob[:, sl], ob[:, sl],
                                                 rin[:, csl], ALU.add)
                    if kind == "gpsimd":
                        eng.wait_ge(p.hd("evac"), ev0 + i + 1)
                        eng.dma_start(
                            out=r5[o * P:(o + 1) * P,
                                   tg * TG + tc * 512:tg * TG + (tc + 1) * 512],
                            in_=ob[:, sl]).then_inc(*p.inc(STp[i % 2], 16))
                    return None

                mm_stage(p, [wffv16], H, D, hid_t, TG, evV, evac_done="vector",
                         use_gpsimd=True, o_group=128)
                es3.close()


def _t(a):
    return np.ascontiguousarray(np.asarray(a, np.float32).T)


def _colmaj(v):
    return np.ascontiguousarray(np.asarray(v, np.float32).reshape(-1, P).T)


def kernel(**inp):
    ncores = 8

    x = np.asarray(inp["L0_attn_x"], np.float32)
    x_cm = []
    for c in range(ncores):
        b, h = divmod(c, 2)
        x_cm.append(_t(x[b, h * T:(h + 1) * T, :]))
    iso = [np.full((1, 1), float(c % 2), np.float32) for c in range(ncores)]

    com = {
        "wg": np.asarray(inp["L0_ffn_w_gate"], np.float32),
        "wu": np.asarray(inp["L0_ffn_w_up"], np.float32),
        "wdn": np.asarray(inp["L0_ffn_w_down"], np.float32),
        "WqT": _t(inp["L0_attn_Wq"]), "WkT0": _t(inp["L0_attn_Wk"]),
        "WvT0": _t(inp["L0_attn_Wv"]), "WoT0": _t(inp["L0_attn_Wo"]),
        "w1T": _t(inp["L1_ffn_w1_w"]), "w2T": _t(inp["L1_ffn_w2_w"]),
        "h1T": _t(inp["L1_ffn_h1_w"]), "h2T": _t(inp["L1_ffn_h2_w"]),
        "w3T": _t(inp["L1_ffn_w3_w"]),
        "s1c": _colmaj(inp["L1_ffn_w1_s"]), "s2c": _colmaj(inp["L1_ffn_w2_s"]),
        "hs1c": _colmaj(inp["L1_ffn_h1_s"]), "hs2c": _colmaj(inp["L1_ffn_h2_s"]),
        "s3": np.ascontiguousarray(
            np.asarray(inp["L1_ffn_w3_s"], np.float32).reshape(1, -1)),
        "WkT1": _t(inp["L1_attn_Wk"]), "WvT1": _t(inp["L1_attn_Wv"]),
        "WrT1": _t(inp["L1_attn_Wr"]), "WoT1": _t(inp["L1_attn_Wo"]),
        "WffkT": _t(inp["L1_attn_Wffk"]), "WffvT": _t(inp["L1_attn_Wffv"]),
        "g1c": _colmaj(inp["L1_attn_ln1_g"]), "b1c": _colmaj(inp["L1_attn_ln1_b"]),
        "g2c": _colmaj(inp["L1_attn_ln2_g"]), "b2c": _colmaj(inp["L1_attn_ln2_b"]),
        "tdc": _colmaj(np.asarray(inp["L1_attn_time_decay"],
                                  np.float32).reshape(-1)),
        "tfc": _colmaj(np.asarray(inp["L1_attn_time_first"],
                                  np.float32).reshape(-1)),
    }

    def run(p, per_core):
        return run_bass_kernel_spmd(p.nc, per_core, core_ids=list(range(ncores)))

    pA = Prog()
    rin = pA.inp("x", [D, T])
    wdA = {"wg": pA.inp("wg", [2816, D]), "wu": pA.inp("wu", [2816, D]),
           "wdn": pA.inp("wdn", [D, 2816])}
    r1o = pA.outp("r1", [D, T])
    pA.consts()
    emit_l0_ffn(pA, rin, r1o, wdA)
    pA.close()
    resA = run(pA, [{"x": x_cm[c], "wg": com["wg"], "wu": com["wu"],
                     "wdn": com["wdn"]} for c in range(ncores)])
    r1 = [np.asarray(resA.results[c]["r1"]) for c in range(ncores)]

    pB = Prog()
    r1i = pB.inp("r1", [D, T])
    wdB = {"WqT": pB.inp("WqT", [D, D]), "WkT": pB.inp("WkT", [D, D]),
           "WvT": pB.inp("WvT", [D, D])}
    qtO = pB.outp("qt", [D, T], BF16)
    nlO = pB.outp("nl", [T, 16 * 65])
    stO = pB.outp("st", [64, 16 * 65])
    pB.consts()
    emit_l0_attn_p1(pB, r1i, qtO, nlO, stO, wdB)
    pB.close()
    resB = run(pB, [{"r1": r1[c], "WqT": com["WqT"], "WkT": com["WkT0"],
                     "WvT": com["WvT0"]} for c in range(ncores)])

    pC = Prog()
    r1i2 = pC.inp("r1", [D, T])
    nlI = pC.inp("nl", [T, 16 * 65])
    qtI = pC.inp("qt", [D, T], BF16)
    stI = pC.inp("st", [64, 16 * 65])
    isoI = pC.inp("iso", [1, 1])
    wdC = {"WoT": pC.inp("WoT", [D, D])}
    r2o = pC.outp("r2", [D, T])
    pC.consts()
    emit_l0_attn_p2(pC, r1i2, r2o, nlI, qtI, stI, isoI, wdC)
    pC.close()
    imC = []
    for c in range(ncores):
        nbr = c - 1 if c % 2 == 1 else c
        imC.append({"r1": r1[c], "nl": np.asarray(resB.results[c]["nl"]),
                    "qt": np.asarray(resB.results[c]["qt"]),
                    "st": np.asarray(resB.results[nbr]["st"]),
                    "iso": iso[c], "WoT": com["WoT0"]})
    resC = run(pC, imC)
    r2 = [np.asarray(resC.results[c]["r2"]) for c in range(ncores)]

    pD = Prog()
    r2i = pD.inp("r2", [D, T])
    wdD0 = {"wg": pD.inp("wg", [2816, D]), "wu": pD.inp("wu", [2816, D]),
            "wdn": pD.inp("wdn", [D, 2816])}
    wdD1 = {"w1T": pD.inp("w1T", [D, 4096]), "w2T": pD.inp("w2T", [D, 4096]),
            "h1T": pD.inp("h1T", [4096, 4096]), "h2T": pD.inp("h2T", [4096, 4096]),
            "w3T": pD.inp("w3T", [4096, D]), "s3": pD.inp("s3", [1, D]),
            "s1c": pD.inp("s1c", [P, 32]), "s2c": pD.inp("s2c", [P, 32]),
            "hs1c": pD.inp("hs1c", [P, 32]), "hs2c": pD.inp("hs2c", [P, 32])}
    r4o = pD.outp("r4", [D, T])
    pD.consts()
    r3s = pD.scratch("r3", [D, T], F32)
    emit_l0_ffn(pD, r2i, r3s, wdD0)
    emit_l1_ffn(pD, r3s, r4o, wdD1)
    pD.close()
    imD = [{"r2": r2[c], "wg": com["wg"], "wu": com["wu"], "wdn": com["wdn"],
            "w1T": com["w1T"], "w2T": com["w2T"], "h1T": com["h1T"],
            "h2T": com["h2T"], "w3T": com["w3T"], "s3": com["s3"],
            "s1c": com["s1c"], "s2c": com["s2c"], "hs1c": com["hs1c"],
            "hs2c": com["hs2c"]} for c in range(ncores)]
    resD = run(pD, imD)
    r4 = [np.asarray(resD.results[c]["r4"]) for c in range(ncores)]

    pE = Prog()
    r4i = pE.inp("r4", [D, T])
    isoE = pE.inp("iso", [1, 1])
    wdE = {"WkT": pE.inp("WkT", [D, D]), "WvT": pE.inp("WvT", [D, D]),
           "WrT": pE.inp("WrT", [D, D]),
           "g1c": pE.inp("g1c", [P, KT_D]), "b1c": pE.inp("b1c", [P, KT_D]),
           "tdc": pE.inp("tdc", [P, KT_D]), "tfc": pE.inp("tfc", [P, KT_D])}
    a32o = pE.outp("a32", [D, T])
    nlE = pE.outp("nl", [D, T])
    dlE = pE.outp("dl", [D, T])
    sgE = pE.outp("sg", [D, T])
    crE = pE.outp("cr", [D, 2])
    pE.consts()
    emit_l1_attn_p1(pE, r4i, a32o, nlE, dlE, sgE, crE, isoE, wdE)
    pE.close()
    imE = [{"r4": r4[c], "iso": iso[c], "WkT": com["WkT1"],
            "WvT": com["WvT1"], "WrT": com["WrT1"], "g1c": com["g1c"],
            "b1c": com["b1c"], "tdc": com["tdc"], "tfc": com["tfc"]}
           for c in range(ncores)]
    resE = run(pE, imE)

    pF = Prog()
    r4i2 = pF.inp("r4", [D, T])
    a32i = pF.inp("a32", [D, T])
    nlF = pF.inp("nl", [D, T])
    dlF = pF.inp("dl", [D, T])
    sgF = pF.inp("sg", [D, T])
    crF = pF.inp("cr", [D, 2])
    isoF = pF.inp("iso", [1, 1])
    wdF = {"WoT": pF.inp("WoT", [D, D]),
           "WffkT": pF.inp("WffkT", [D, 4096]),
           "WffvT": pF.inp("WffvT", [4096, D]),
           "g2c": pF.inp("g2c", [P, KT_D]), "b2c": pF.inp("b2c", [P, KT_D]),
           "tdc": pF.inp("tdc", [P, KT_D])}
    wdF1 = {"w1T": pF.inp("w1T", [D, 4096]), "w2T": pF.inp("w2T", [D, 4096]),
            "h1T": pF.inp("h1T", [4096, 4096]), "h2T": pF.inp("h2T", [4096, 4096]),
            "w3T": pF.inp("w3T", [4096, D]), "s3": pF.inp("s3", [1, D]),
            "s1c": pF.inp("s1c", [P, 32]), "s2c": pF.inp("s2c", [P, 32]),
            "hs1c": pF.inp("hs1c", [P, 32]), "hs2c": pF.inp("hs2c", [P, 32])}
    yO = pF.outp("y", [D, T])
    pF.consts()
    r5s = pF.scratch("r5", [D, T], F32)
    emit_l1_attn_p2(pF, r4i2, r5s, a32i, nlF, dlF, sgF, crF, isoF, wdF)
    emit_l1_ffn(pF, r5s, yO, wdF1)
    pF.close()
    imF = []
    for c in range(ncores):
        nbr = c - 1 if c % 2 == 1 else c
        imF.append({"r4": r4[c], "a32": np.asarray(resE.results[c]["a32"]),
                    "nl": np.asarray(resE.results[c]["nl"]),
                    "dl": np.asarray(resE.results[c]["dl"]),
                    "sg": np.asarray(resE.results[c]["sg"]),
                    "cr": np.asarray(resE.results[nbr]["cr"]),
                    "iso": iso[c], "WoT": com["WoT1"],
                    "WffkT": com["WffkT"], "WffvT": com["WffvT"],
                    "g2c": com["g2c"], "b2c": com["b2c"], "tdc": com["tdc"],
                    "w1T": com["w1T"], "w2T": com["w2T"], "h1T": com["h1T"],
                    "h2T": com["h2T"], "w3T": com["w3T"], "s3": com["s3"],
                    "s1c": com["s1c"], "s2c": com["s2c"],
                    "hs1c": com["hs1c"], "hs2c": com["hs2c"]})
    resF = run(pF, imF)

    y = np.empty((4, 2 * T, D), np.float32)
    for c in range(ncores):
        b, h = divmod(c, 2)
        y[b, h * T:(h + 1) * T, :] = np.asarray(resF.results[c]["y"]).T
    return y
